# version 20
# baseline (speedup 1.0000x reference)
import contextlib
import numpy as np
import ml_dtypes
import concourse.bass as bass
import concourse.mybir as mybir
import concourse.bass_utils as _bass_utils
from concourse.bass_utils import run_bass_kernel_spmd

if not getattr(_bass_utils, "_ldwopt_patched", False):
    _ORIG_RUN_COMMAND = _bass_utils.run_command

    def _run_command_ldwopt(argv, **kwargs):
        argv = ["--enable-ldw-opt=true" if a == "--enable-ldw-opt=false" else a
                for a in argv]
        return _ORIG_RUN_COMMAND(argv, **kwargs)

    _bass_utils.run_command = _run_command_ldwopt
    _bass_utils._ldwopt_patched = True

F32 = mybir.dt.float32
BF16 = mybir.dt.bfloat16
U8 = mybir.dt.uint8
AX = mybir.AxisListType
ALU = mybir.AluOpType
AF = mybir.ActivationFunctionType
BF = ml_dtypes.bfloat16

NCORES = 8
SEQS = 2
T = 2048
D = 1024
NCH = 8
TW = 512
NTT = T // TW

ALL_BLOCKS = ("l0a", "l0f", "l1a", "l1f")


QUEUES = ("pe", "act", "dve", "pool", "sp")


class Buf:
    __slots__ = ("w", "rs")

    def __init__(self):
        self.w = None
        self.rs = []

    def rdep(self):
        return [self.w] if self.w else []

    def wdep(self):
        return ([self.w] if self.w else []) + self.rs

    def wrote(self, tok):
        assert tok is not None
        self.w = tok
        self.rs = []

    def read(self, tok):
        assert tok is not None
        self.rs.append(tok)


class R:
    __slots__ = ("t", "buf")

    def __init__(self, t):
        self.t = t
        self.buf = Buf()


class Sched:

    def __init__(self, nc, stack):
        self.nc = nc
        self.q = {e: [] for e in QUEUES}
        names = ("pe", "act", "dve", "pool", "sp", "dsp", "dpool")
        self.sem = {k: stack.enter_context(nc.semaphore(f"s_{k}")) for k in names}
        self.cnt = {k: 0 for k in names}
        self.waited = {e: {} for e in QUEUES}
        self.dma_issued = {"sp": 0, "pool": 0}
        self.dma_flushed = {"sp": 0, "pool": 0}
        self.dma_flush_tok = {"sp": None, "pool": None}

    def _dsem(self, qname):
        return "dsp" if qname == "sp" else "dpool"

    def _flush_dma(self, qname, upto):
        if self.dma_flushed[qname] >= upto:
            return self.dma_flush_tok[qname]
        dsem = self._dsem(qname)
        issued = self.dma_issued[qname]
        q = self.q[qname]
        want = 16 * issued
        if want > self.waited[qname].get(dsem, 0):
            q.append(("w", dsem, want))
            self.waited[qname][dsem] = want
        own = qname
        self.cnt[own] += 1
        sem = self.sem[own]
        q.append(("d", lambda e, sem=sem: e.sem_inc(sem, 1)))
        tok = (own, self.cnt[own])
        self.dma_flushed[qname] = issued
        self.dma_flush_tok[qname] = tok
        return tok

    def _resolve(self, d):
        if d is None or d[0] != "dma":
            return d
        _, qname, idx = d
        return self._flush_dma(qname, idx)

    def emit(self, eng, fn, deps=(), sem_name=None, inc=1):
        q = self.q[eng]
        for d in deps:
            d = self._resolve(d)
            if d is None:
                continue
            s, v = d
            if s == eng:
                continue
            if v > self.waited[eng].get(s, 0):
                q.append(("w", s, v))
                self.waited[eng][s] = v
        if inc:
            s = sem_name or eng
            self.cnt[s] += inc
            q.append(("i", fn, s, inc))
            return (s, self.cnt[s])
        q.append(("d", fn))
        return None

    def op(self, eng, fn, reads=(), writes=(), extra=(), sem_name=None):
        deps = list(extra)
        for r in reads:
            deps += r.buf.rdep()
        for r in writes:
            deps += r.buf.wdep()
        tok = self.emit(eng, fn, deps, sem_name, 1)
        for r in reads:
            r.buf.read(tok)
        for r in writes:
            r.buf.wrote(tok)
        return tok

    def dma(self, qname, out, in_, reads=(), writes=(), extra=()):
        deps = list(extra)
        for r in reads:
            deps += r.buf.rdep()
        for r in writes:
            deps += r.buf.wdep()
        dsem = self._dsem(qname)
        q = self.q[qname]
        for d in deps:
            d = self._resolve(d)
            if d is None:
                continue
            s, v = d
            if s == qname:
                continue
            if v > self.waited[qname].get(s, 0):
                q.append(("w", s, v))
                self.waited[qname][s] = v
        self.cnt[dsem] += 16
        q.append(("i", lambda e, out=out, in_=in_: e.dma_start(out=out, in_=in_),
                  dsem, 16))
        self.dma_issued[qname] += 1
        tok = ("dma", qname, self.dma_issued[qname])
        for r in reads:
            r.buf.read(tok)
        for r in writes:
            r.buf.wrote(tok)
        return tok

    def mm_group(self, psum_r, items, extra=()):
        n = len(items)
        readset = []
        tok = None
        for i, (pap, l, rr, reads) in enumerate(items):
            d = []
            if i == 0:
                d += list(extra) + psum_r.buf.wdep()
            for x in reads:
                d += x.buf.rdep()
                readset.append(x)
            tok = self.emit(
                "pe",
                lambda e, pap=pap, l=l, rr=rr, f=(i == 0), s=(i == n - 1): e.matmul(
                    pap, l, rr, start=f, stop=s),
                deps=d, inc=(1 if i == n - 1 else 0))
        for x in readset:
            x.buf.read(tok)
        psum_r.buf.wrote(tok)
        return tok

    def barrier(self):
        for qn in ("sp", "pool"):
            if self.dma_issued[qn] > self.dma_flushed[qn]:
                self._flush_dma(qn, self.dma_issued[qn])
        snap = dict(self.cnt)
        for e in QUEUES:
            for s, v in snap.items():
                if v > self.waited[e].get(s, 0):
                    self.q[e].append(("w", s, v))
                    self.waited[e][s] = v

    def replay(self, block):
        def mk(name):
            def body(eng):
                for it in self.q[name]:
                    if it[0] == "w":
                        eng.wait_ge(self.sem[it[1]], it[2])
                    elif it[0] == "d":
                        it[1](eng)
                    else:
                        it[1](eng).then_inc(self.sem[it[2]], it[3])
            return body
        block.tensor(mk("pe"))
        block.scalar(mk("act"))
        block.vector(mk("dve"))
        block.gpsimd(mk("pool"))
        block.sync(mk("sp"))


def cdma(sc, ctx, dst_r, dst_ap, f32_ap, scr_ap, key, q="pool"):
    if scr_ap is None:
        sc.dma(q, dst_ap, f32_ap, writes=[dst_r])
        return
    wb = ctx["wsbuf"].setdefault(key, Buf())
    if key in ctx["wcache"]:
        tok = sc.dma(q, dst_ap, scr_ap, writes=[dst_r], extra=wb.rdep())
        wb.read(tok)
    else:
        sc.dma(q, dst_ap, f32_ap, writes=[dst_r])
        tok = sc.dma("pool", scr_ap, f32_ap, extra=wb.wdep())
        wb.wrote(tok)
        ctx["wcache"].add(key)


class PsumPool:
    def __init__(self, nc, stack):
        self.r = [R(stack.enter_context(nc.psum_tensor(f"pb{i}", [128, 512], F32)))
                  for i in range(8)]
        self.i = 0

    def take(self, exclude=()):
        for _ in range(8):
            r = self.r[self.i]
            self.i = (self.i + 1) % 8
            if r not in exclude:
                return r
        raise RuntimeError("psum pool exhausted")


class Alloc:

    def __init__(self, nc, sc):
        self.nc = nc
        self.sc = sc
        self.n = 0

    @contextlib.contextmanager
    def scope(self):
        with contextlib.ExitStack() as st:
            def mk(name, cols, dt=F32, parts=128):
                self.n += 1
                t = st.enter_context(
                    self.nc.sbuf_tensor(f"{name}{self.n}", [parts, cols], dt))
                return R(t)
            yield mk
        self.sc.barrier()



def _lhsT(w):
    din, dout = w.shape
    assert din % 128 == 0
    return np.ascontiguousarray(
        np.asarray(w, dtype=np.float32).reshape(din // 128, 128, dout))


def _lhsT_oc(w):
    w = np.asarray(w, dtype=np.float32)
    din, dout = w.shape
    KC, OC = din // 128, dout // 128
    return np.ascontiguousarray(
        w.reshape(KC, 128, OC, 128).transpose(2, 1, 0, 3).reshape(OC, 128, din))


def _ppv(v):
    v = np.asarray(v, dtype=np.float32)
    n = v.shape[0]
    assert n % 128 == 0
    return np.ascontiguousarray(v.reshape(n // 128, 128).T)


def _pad_cols(w, cols):
    out = np.zeros((w.shape[0], cols), np.float32)
    out[:, :w.shape[1]] = w
    return out


def _pad_rows(w, rows):
    out = np.zeros((rows, w.shape[1]), np.float32)
    out[:w.shape[0]] = w
    return out


def make_consts():
    c = {}
    i = np.arange(128)[:, None]
    cc = np.arange(896)[None, :]
    c["mst_l0"] = ((cc - 384) >= i).astype(BF)
    cc2 = np.arange(1408)[None, :]
    c["mst_l1"] = (((cc2 >= i + 384) & (cc2 <= i + 896))).astype(np.uint8)
    e2 = np.zeros((128, 2), BF)
    e2[0:64, 0] = 1
    e2[64:128, 1] = 1
    c["hh_ind"] = e2
    eb = np.zeros((2, 128), BF)
    eb[0, 0:64] = 1
    eb[1, 64:128] = 1
    c["hh_bcast"] = eb
    P1 = np.zeros((64, 64), np.float32)
    for d in range(32):
        P1[d, d + 32] = -1.0
        P1[d + 32, d] = 1.0
    P2 = np.zeros((128, 128), np.float32)
    P2[0:64, 0:64] = P1
    P2[64:128, 64:128] = P1
    c["rotm"] = np.ascontiguousarray(P2.T).astype(BF)
    half = np.arange(0, 64, 2, dtype=np.float64)
    inv_freq = 1.0 / 10000.0 ** (half / 64.0)
    t = np.arange(T, dtype=np.float64)
    freqs = t[:, None] * inv_freq[None, :]
    freqs = np.concatenate([freqs, freqs], axis=1)
    base = (half + 0.4 * 64) / (1.4 * 64)
    power = (t - T // 2) / 256.0
    scale = base[None, :] ** power[:, None]
    scale = np.concatenate([scale, scale], axis=1)
    cos, sin = np.cos(freqs), np.sin(freqs)
    def tab(m):
        m = np.asarray(m, np.float32).T
        return np.ascontiguousarray(np.concatenate([m, m], axis=0))
    c["t1q"] = tab(cos * scale)
    c["t2q"] = tab(sin * scale)
    c["t1k"] = tab(cos / scale)
    c["t2k"] = tab(sin / scale)
    return c


def prep_weights(iv):
    w = {}
    w["wq"] = _lhsT(np.asarray(iv["L0_attn_Wq_w"]).T)
    w["wk"] = _lhsT(np.asarray(iv["L0_attn_Wk_w"]).T)
    w["wv"] = _lhsT(np.asarray(iv["L0_attn_Wv_w"]).T)
    w["wo"] = _lhsT(np.asarray(iv["L0_attn_Wo_w"]).T)
    w["b_l0a"] = np.ascontiguousarray(np.concatenate(
        [_ppv(iv[f"L0_attn_W{m}_b"]) for m in "qkvo"], axis=1))
    w["bv_row"] = np.asarray(iv["L0_attn_Wv_b"], np.float32).reshape(1, D)
    w["f_w1"] = _lhsT_oc(np.asarray(iv["L0_ffn_w1_w"]).T)
    w["f_w2"] = _lhsT_oc(np.asarray(iv["L0_ffn_w2_w"]).T)
    w["f_h1"] = _lhsT_oc(np.asarray(iv["L0_ffn_h1_w"]).T)
    w["f_h2"] = _lhsT_oc(np.asarray(iv["L0_ffn_h2_w"]).T)
    w["f_w3"] = _lhsT_oc(np.asarray(iv["L0_ffn_w3_w"]).T)
    w["f_s1"] = _ppv(iv["L0_ffn_w1_s"])
    w["f_s2"] = _ppv(iv["L0_ffn_w2_s"])
    w["f_sh1"] = _ppv(iv["L0_ffn_h1_s"])
    w["f_sh2"] = _ppv(iv["L0_ffn_h2_s"])
    w["f_s3"] = _ppv(iv["L0_ffn_w3_s"])
    w["a_win"] = _lhsT_oc(iv["L1_attn_W_in"])
    w["a_wx"] = _lhsT(iv["L1_attn_W_x"])
    w["a_wdt"] = _lhsT(iv["L1_attn_W_dt"])
    w["a_wout"] = _lhsT(iv["L1_attn_W_out"])
    w["a_bin"] = _ppv(iv["L1_attn_b_in"])
    w["a_bx"] = _ppv(iv["L1_attn_b_x"])
    w["a_bdt"] = _ppv(iv["L1_attn_b_dt"])
    w["a_bout"] = _ppv(iv["L1_attn_b_out"])
    w["a_wnorm"] = _ppv(iv["L1_attn_w_norm"])
    wqkv = np.asarray(iv["L1_ffn_wqkv"], np.float32)
    w["g_wqkv_qk"] = np.stack([_lhsT_oc(wqkv[l][:, 0:1024]) for l in range(2)])
    w["g_wqkv_v"] = np.stack([_lhsT(wqkv[l][:, 1024:1536]) for l in range(2)])
    wo = np.asarray(iv["L1_ffn_wo"], np.float32)
    w["g_wo"] = np.stack([_lhsT_oc(wo[l]) for l in range(2)])
    w1 = np.asarray(iv["L1_ffn_w1"], np.float32)
    HID = 2730
    w["g_w1a"] = np.stack([_lhsT_oc(_pad_cols(w1[l][:, :HID], 2816)) for l in range(2)])
    w["g_w1g"] = np.stack([_lhsT_oc(_pad_cols(w1[l][:, HID:], 2816)) for l in range(2)])
    w2 = np.asarray(iv["L1_ffn_w2"], np.float32)
    w["g_w2"] = np.stack([_lhsT_oc(_pad_rows(w2[l], 2816)) for l in range(2)])
    b1 = np.asarray(iv["L1_ffn_b1"], np.float32)
    w["g_b1a"] = np.stack([_ppv(_pad_cols(b1[l][None, :HID], 2816)[0]) for l in range(2)])
    w["g_b1g"] = np.stack([_ppv(_pad_cols(b1[l][None, HID:], 2816)[0]) for l in range(2)])
    w["g_b2"] = np.stack([_ppv(np.asarray(iv["L1_ffn_b2"])[l]) for l in range(2)])
    w["g_lng"] = np.stack([_ppv(np.asarray(iv["L1_ffn_ln_g"])[l]) for l in range(2)])
    w["g_lnb"] = np.stack([_ppv(np.asarray(iv["L1_ffn_ln_b"])[l]) for l in range(2)])
    qs = np.asarray(iv["L1_ffn_q_scale"], np.float32)
    ks = np.asarray(iv["L1_ffn_k_scale"], np.float32)
    sig = np.arange(64)
    sig = np.concatenate([sig[32:], sig[:32]])
    w["g_qs"] = np.stack([np.tile(qs[l], 2).reshape(128, 1) for l in range(2)])
    w["g_ks"] = np.stack([np.tile(ks[l], 2).reshape(128, 1) for l in range(2)])
    w["g_qss"] = np.stack([np.tile(qs[l][sig], 2).reshape(128, 1) for l in range(2)])
    w["g_kss"] = np.stack([np.tile(ks[l][sig], 2).reshape(128, 1) for l in range(2)])
    w["g_qs_row"] = np.ascontiguousarray(qs.reshape(2, 1, 64))
    w["g_ks_row"] = np.ascontiguousarray(ks.reshape(2, 1, 64))
    for k in w:
        w[k] = np.ascontiguousarray(w[k], dtype=np.float32) if w[k].dtype != np.uint8 else w[k]
    return w



def ln_stats_tile(sc, ctx, bf_fn, want_mean=True, eps_name="e5"):
    pp, ones = ctx["pp"], ctx["ones_col_bf"]
    st = ctx["lnsets"][ctx["ln_i"] % len(ctx["lnsets"])]
    ctx["ln_i"] += 1
    sq = st["sq"]
    ps, pq = pp.take(), pp.take()
    r_mu, r_tmp, r_rstd = st["mu"], st["tmp"], st["rstd"]
    mm_s = mm_q = None
    for c in range(NCH):
        ap, reads = bf_fn(c)
        slot = sq[c % 2]
        tok_sq = sc.op("act", lambda e, slot=slot, ap=ap: e.activation(
            slot.t[:], ap, AF.Square), reads=list(reads), writes=[slot])
        if want_mean:
            d = []
            for r in reads:
                d += r.buf.rdep()
            if c == 0:
                d += ps.buf.wdep()
            mm_s = sc.emit("pe", lambda e, ap=ap, f=(c == 0), s=(c == NCH - 1):
                           e.matmul(ps.t[0:1, 0:TW], ones.t[:], ap, start=f, stop=s),
                           deps=d, inc=1)
            for r in reads:
                r.buf.read(mm_s)
        d = [tok_sq] + (pq.buf.wdep() if c == 0 else [])
        mm_q = sc.emit("pe", lambda e, slot=slot, f=(c == 0), s=(c == NCH - 1):
                       e.matmul(pq.t[0:1, 0:TW], ones.t[:], slot.t[:], start=f, stop=s),
                       deps=d, inc=1)
        slot.buf.read(mm_q)
    if want_mean:
        ps.buf.wrote(mm_s)
    pq.buf.wrote(mm_q)
    if want_mean:
        sc.op("dve", lambda e: e.tensor_scalar(
            r_mu.t[:], ps.t[0:1, 0:TW], 1.0 / D, None, ALU.mult),
            reads=[ps], writes=[r_mu])
        sc.op("dve", lambda e: e.tensor_tensor(
            r_tmp.t[:], r_mu.t[:], r_mu.t[:], ALU.mult),
            reads=[r_mu], writes=[r_tmp])
        sc.op("dve", lambda e: e.scalar_tensor_tensor(
            r_tmp.t[:], pq.t[0:1, 0:TW], 1.0 / D, r_tmp.t[:], ALU.mult,
            ALU.subtract), reads=[pq], writes=[r_tmp])
    else:
        sc.op("dve", lambda e: e.tensor_scalar(
            r_tmp.t[:], pq.t[0:1, 0:TW], 1.0 / D, None, ALU.mult),
            reads=[pq], writes=[r_tmp])
    eps = ctx[eps_name]
    sc.op("act", lambda e: e.activation(
        r_rstd.t[:], r_tmp.t[:], AF.Ln, bias=eps.t[:]),
        reads=[r_tmp, eps], writes=[r_rstd])
    sc.op("act", lambda e: e.activation(
        r_rstd.t[:], r_rstd.t[:], AF.Exp, scale=-0.5), writes=[r_rstd])
    return (r_mu if want_mean else None), r_rstd


BCAST_MODE = "hilo"


def bcast_row(sc, ctx, row_r, row_ap=None, width=TW):
    pp = ctx["pp"]
    pb = pp.take()
    ap = row_ap if row_ap is not None else row_r.t[:]
    if BCAST_MODE == "fp32":
        sc.op("pe", lambda e: e.matmul(pb.t[:, 0:width], ctx["ones_row_f"].t[:],
                                       ap, start=True, stop=True),
              reads=[row_r, ctx["ones_row_f"]], writes=[pb])
        return pb
    hi, lo = ctx["bcsets"][ctx["bc_i"] % len(ctx["bcsets"])]
    ctx["bc_i"] += 1
    sc.op("dve", lambda e: e.tensor_copy(hi.t[0:1, 0:width], ap),
          reads=[row_r], writes=[hi])
    if BCAST_MODE == "hilo":
        sc.op("dve", lambda e: e.tensor_tensor(
            lo.t[0:1, 0:width], ap, hi.t[0:1, 0:width], ALU.subtract),
            reads=[row_r, hi], writes=[lo])
        items = [(pb.t[:, 0:width], ctx["ones_row_bf"].t[:], hi.t[0:1, 0:width],
                  [hi, ctx["ones_row_bf"]]),
                 (pb.t[:, 0:width], ctx["ones_row_bf"].t[:], lo.t[0:1, 0:width],
                  [lo])]
    else:
        items = [(pb.t[:, 0:width], ctx["ones_row_bf"].t[:], hi.t[0:1, 0:width],
                  [hi, ctx["ones_row_bf"]])]
    sc.mm_group(pb, items)
    return pb


def load_xtile(sc, ctx, src_ap, s, t0, dst=None):
    xt = dst if dst is not None else ctx["xt"]
    src = src_ap[s].rearrange("c p t -> p c t")[:, :, t0:t0 + TW]
    d = xt.t[:, 0:NCH * TW].rearrange("p (c t) -> p c t", c=NCH)
    sc.dma("sp", d, src, writes=[xt])
    return xt


def store_xtile(sc, ctx, dst_ap, s, t0, src_r):
    dst = dst_ap[s].rearrange("c p t -> p c t")[:, :, t0:t0 + TW]
    sap = src_r.t[:, 0:NCH * TW].rearrange("p (c t) -> p c t", c=NCH)
    sc.dma("sp", dst, sap, reads=[src_r])


def cast_x_tile(sc, ctx, xt):
    cs = ctx["cast_stage"]
    for c in range(NCH):
        sc.op("dve", lambda e, c=c: e.tensor_copy(
            cs.t[:, c * TW:(c + 1) * TW], xt.t[:, c * TW:(c + 1) * TW]),
            reads=[xt], writes=[cs])
    return cs



def emit_l0a(sc, ctx, al, s, src_ap, dst_ap, wd):
    pp = ctx["pp"]
    with al.scope() as mk:
        xs = mk("xs", NCH * T, BF16)
        khg = mk("khg", 4 * T, BF16)
        vhg = mk("vhg", 16 * 520, BF16)
        qt = mk("qt", 4 * TW, BF16)
        ao = mk("ao", NCH * T, BF16)
        wst_b = [mk(f"wst{i}", 8 * 512, BF16) for i in range(2)]
        e_b = [mk(f"e{i}", TW, BF16) for i in range(4)]
        em_b = [mk(f"em{i}", TW, BF16) for i in range(4)]
        bvr = mk("bvr", D, BF16, parts=1)
        bia = mk("bia", 32)
        sc.dma("pool", bvr.t[:], wd["bv_row"], writes=[bvr])
        sc.dma("sp", bia.t[:], wd["b_l0a"], writes=[bia])

        for tt in range(NTT):
            t0 = tt * TW
            xt = load_xtile(sc, ctx, src_ap, s, t0)
            cs = cast_x_tile(sc, ctx, xt)

            def bf_fn(c, cs=cs):
                return cs.t[:, c * TW:(c + 1) * TW], [cs]
            r_mu, r_rstd = ln_stats_tile(sc, ctx, bf_fn, want_mean=True)
            pmu = bcast_row(sc, ctx, r_mu)
            pa = bcast_row(sc, ctx, r_rstd)
            for c in range(NCH):
                k = c % 2
                tf = ctx["tmp_f"][k]
                sc.op("dve", lambda e, c=c, tf=tf, xt=xt, pmu=pmu: e.tensor_tensor(
                    tf.t[:], xt.t[:, c * TW:(c + 1) * TW], pmu.t[:, 0:TW],
                    ALU.subtract), reads=[xt, pmu], writes=[tf])
                sc.op("dve", lambda e, c=c, tf=tf, t0=t0, pa=pa: e.tensor_tensor(
                    xs.t[:, c * T + t0:c * T + t0 + TW], tf.t[:], pa.t[:, 0:TW],
                    ALU.mult), reads=[tf, pa], writes=[xs])

        if s == 0:
            ctx["dump"]("xs", xs.t[:], [xs])

        for hg in range(2):
            cols = slice(hg * 512, hg * 512 + 512)
            wst = wst_b[0]
            for kc in range(NCH):
                sc.dma("pool", wst.t[:, kc * 512:(kc + 1) * 512],
                       wd["wk"][kc][:, cols], writes=[wst])
            for oc4 in range(4):
                oc = hg * 4 + oc4
                for tt in range(NTT):
                    pb = pp.take()
                    items = [(pb.t[:, 0:TW],
                              wst.t[:, kc * 512 + oc4 * 128: kc * 512 + oc4 * 128 + 128],
                              xs.t[:, kc * T + tt * TW: kc * T + (tt + 1) * TW],
                              [wst, xs]) for kc in range(NCH)]
                    sc.mm_group(pb, items)
                    sc.op("dve", lambda e, pb=pb, oc4=oc4, tt=tt, oc=oc:
                          e.tensor_scalar(
                              khg.t[:, oc4 * T + tt * TW: oc4 * T + (tt + 1) * TW],
                              pb.t[:, 0:TW], 1.0, bia.t[:, 8 + oc:9 + oc],
                              ALU.mult, ALU.add),
                          reads=[pb, bia], writes=[khg])
            wst = wst_b[1]
            for kc in range(NCH):
                sc.dma("pool", wst.t[:, kc * 512:(kc + 1) * 512],
                       wd["wv"][kc][:, cols], writes=[wst])
            onesap = vhg.t[:, 0:16 * 520].rearrange(
                "p (tc h o) -> p tc h o", tc=16, h=8)[:, :, :, 64:65]
            sc.op("dve", lambda e: e.memset(onesap, 1.0), writes=[vhg])
            for tc in range(16):
                pb = pp.take()
                items = [(pb.t[:, 0:512],
                          xs.t[:, kc * T + tc * 128: kc * T + tc * 128 + 128],
                          wst.t[:, kc * 512:(kc + 1) * 512],
                          [wst, xs]) for kc in range(NCH)]
                items.append((pb.t[:, 0:512], ctx["ones_row_bf"].t[:],
                              bvr.t[0:1, hg * 512: hg * 512 + 512], [bvr]))
                sc.mm_group(pb, items)
                d = vhg.t[:, tc * 520: tc * 520 + 520].rearrange(
                    "p (h o) -> p h o", h=8)[:, :, 0:64]
                srcap = pb.t[:, 0:512].rearrange("p (h o) -> p h o", h=8)
                sc.op("dve", lambda e, d=d, srcap=srcap: e.tensor_copy(d, srcap),
                      reads=[pb], writes=[vhg])
            if s == 0 and hg == 0:
                ctx["dump"]("khg0", khg.t[:], [khg])
                ctx["dump"]("vhg0", vhg.t[:], [vhg])
            wst = wst_b[0]
            for kc in range(NCH):
                sc.dma("pool", wst.t[:, kc * 512:(kc + 1) * 512],
                       wd["wq"][kc][:, cols], writes=[wst])
            for qi in range(NTT):
                for oc4 in range(4):
                    oc = hg * 4 + oc4
                    pb = pp.take()
                    items = [(pb.t[:, 0:TW],
                              wst.t[:, kc * 512 + oc4 * 128: kc * 512 + oc4 * 128 + 128],
                              xs.t[:, kc * T + qi * TW: kc * T + (qi + 1) * TW],
                              [wst, xs]) for kc in range(NCH)]
                    sc.mm_group(pb, items)
                    sc.op("dve", lambda e, pb=pb, oc4=oc4, oc=oc:
                          e.tensor_scalar(
                              qt.t[:, oc4 * TW:(oc4 + 1) * TW], pb.t[:, 0:TW],
                              1.0, bia.t[:, oc:oc + 1], ALU.mult, ALU.add),
                          reads=[pb, bia], writes=[qt])
                if s == 0 and hg == 0 and qi == 0:
                    ctx["dump"]("qt00", qt.t[:], [qt])
                for hh in range(8):
                    h = hg * 8 + hh
                    rbase = (hh % 2) * 64
                    kcc = hh // 2
                    pav = pp.take()
                    nkj = 4 * qi + 4
                    for b0 in range(0, nkj, 4):
                        batch = list(range(b0, min(b0 + 4, nkj)))
                        stiles = []
                        for kj in batch:
                            pbs = pp.take(exclude=[pav])
                            kap = khg.t[rbase:rbase + 64,
                                        kcc * T + kj * 128: kcc * T + kj * 128 + 128]
                            qap = qt.t[rbase:rbase + 64, kcc * TW: (kcc + 1) * TW]
                            sc.mm_group(pbs, [(pbs.t[:, 0:TW], kap, qap, [khg, qt])])
                            stiles.append((kj, pbs))
                        esrcs = []
                        for kj, pbs in stiles:
                            eb = e_b[kj % 4]
                            sc.op("act", lambda e, pbs=pbs, eb=eb: e.activation(
                                eb.t[:], pbs.t[:, 0:TW], AF.Exp, scale=1.0 / 32.0),
                                reads=[pbs], writes=[eb])
                            if kj >= 4 * qi:
                                r = kj - 4 * qi
                                off = 384 - 128 * r
                                em = em_b[kj % 4]
                                sc.op("dve", lambda e, eb=eb, off=off, em=em:
                                      e.tensor_tensor(em.t[:], eb.t[:],
                                                      ctx["mst0"].t[:, off:off + TW],
                                                      ALU.mult),
                                      reads=[eb, ctx["mst0"]], writes=[em])
                                esrcs.append((kj, em))
                            else:
                                esrcs.append((kj, eb))
                        for kj, esrc in esrcs:
                            vap = vhg.t[:, kj * 520 + hh * 65: kj * 520 + hh * 65 + 65]
                            d = [] if kj else list(pav.buf.wdep())
                            d += vhg.buf.rdep() + esrc.buf.rdep()
                            tok = sc.emit("pe", lambda e, vap=vap, esrc=esrc,
                                          pav=pav, f=(kj == 0),
                                          st=(kj == nkj - 1): e.matmul(
                                              pav.t[0:65, 0:TW], vap, esrc.t[:],
                                              start=f, stop=st),
                                          deps=d, inc=1)
                            vhg.buf.read(tok)
                            esrc.buf.read(tok)
                            if kj == nkj - 1:
                                pav.buf.wrote(tok)
                    r_rec = ctx["r_recs"][hh % 3]
                    sc.op("act", lambda e, pav=pav, r_rec=r_rec: e.activation(
                        r_rec.t[:], pav.t[64:65, 0:TW], AF.Ln),
                        reads=[pav], writes=[r_rec])
                    sc.op("act", lambda e, r_rec=r_rec: e.activation(
                        r_rec.t[:], r_rec.t[:], AF.Exp, scale=-1.0),
                        writes=[r_rec])
                    pbc = bcast_row(sc, ctx, r_rec)
                    tf = ctx["tmp_f"][hh % 4]
                    sc.op("dve", lambda e, pav=pav, tf=tf: e.tensor_copy(
                        tf.t[0:64, :], pav.t[0:64, 0:TW]),
                        reads=[pav], writes=[tf])
                    aod = ao.t[rbase:rbase + 64,
                               (hg * 4 + kcc) * T + qi * TW: (hg * 4 + kcc) * T + (qi + 1) * TW]
                    sc.op("dve", lambda e, tf=tf, aod=aod, pbc=pbc: e.tensor_tensor(
                        aod, tf.t[0:64, :], pbc.t[0:64, 0:TW], ALU.mult),
                        reads=[tf, pbc], writes=[ao])

        if s == 0:
            ctx["dump"]("ao", ao.t[:], [ao])
        for kc in range(NCH):
            b = wst_b[kc // 4]
            sc.dma("pool", b.t[:, (kc % 4) * 1024:(kc % 4 + 1) * 1024],
                   wd["wo"][kc], writes=[b])
        for tt in range(NTT):
            t0 = tt * TW
            xt = load_xtile(sc, ctx, src_ap, s, t0)
            for oc in range(NCH):
                pb = pp.take()
                items = [(pb.t[:, 0:TW],
                          wst_b[kc // 4].t[:, (kc % 4) * 1024 + oc * 128:
                                           (kc % 4) * 1024 + oc * 128 + 128],
                          ao.t[:, kc * T + t0: kc * T + t0 + TW],
                          [wst_b[kc // 4], ao]) for kc in range(NCH)]
                sc.mm_group(pb, items)
                sc.op("dve", lambda e, pb=pb, oc=oc: e.scalar_tensor_tensor(
                    xt.t[:, oc * TW:(oc + 1) * TW], pb.t[:, 0:TW],
                    bia.t[:, 24 + oc:25 + oc], xt.t[:, oc * TW:(oc + 1) * TW],
                    ALU.add, ALU.add), reads=[pb, bia], writes=[xt])
            store_xtile(sc, ctx, dst_ap, s, t0, xt)



WEIGHT_SHAPES = {
    "wq": [8, 128, 1024], "wk": [8, 128, 1024], "wv": [8, 128, 1024],
    "wo": [8, 128, 1024], "b_l0a": [128, 32], "bv_row": [1, 1024],
    "f_w1": [32, 128, 1024], "f_w2": [32, 128, 1024],
    "f_h1": [32, 128, 4096], "f_h2": [32, 128, 4096], "f_w3": [8, 128, 4096],
    "f_s1": [128, 32], "f_s2": [128, 32], "f_sh1": [128, 32],
    "f_sh2": [128, 32], "f_s3": [128, 8],
    "a_win": [6, 128, 1024], "a_wx": [2, 128, 256], "a_wdt": [2, 128, 256],
    "a_wout": [2, 128, 1024], "a_bin": [128, 6], "a_bx": [128, 2],
    "a_bdt": [128, 2], "a_bout": [128, 8], "a_wnorm": [128, 8],
    "g_wqkv_qk": [2, 8, 128, 1024], "g_wqkv_v": [2, 8, 128, 512],
    "g_wo": [2, 8, 128, 512],
    "g_w1a": [2, 22, 128, 1024], "g_w1g": [2, 22, 128, 1024],
    "g_w2": [2, 8, 128, 2816],
    "g_b1a": [2, 128, 22], "g_b1g": [2, 128, 22], "g_b2": [2, 128, 8],
    "g_lng": [2, 128, 8], "g_lnb": [2, 128, 8],
    "g_qs": [2, 128, 1], "g_ks": [2, 128, 1],
    "g_qss": [2, 128, 1], "g_kss": [2, 128, 1],
    "g_qs_row": [2, 1, 64], "g_ks_row": [2, 1, 64],
}
CONST_SHAPES = {
    "mst_l0": ([128, 896], BF16), "mst_l1": ([128, 1408], U8),
    "hh_ind": ([128, 2], BF16), "hh_bcast": ([2, 128], BF16),
    "rotm": ([128, 128], BF16),
    "t1q": ([128, T], F32), "t2q": ([128, T], F32),
    "t1k": ([128, T], F32), "t2k": ([128, T], F32),
}
BLOCK_WEIGHTS = {
    "l0a": ["wq", "wk", "wv", "wo", "b_l0a", "bv_row"],
    "l0f": ["f_w1", "f_w2", "f_h1", "f_h2", "f_w3",
            "f_s1", "f_s2", "f_sh1", "f_sh2", "f_s3"],
    "l1a": ["a_win", "a_wx", "a_wdt", "a_wout", "a_bin", "a_bx", "a_bdt",
            "a_bout", "a_wnorm"],
    "l1f": ["g_wqkv_qk", "g_wqkv_v", "g_wo", "g_w1a", "g_w1g", "g_w2", "g_b1a", "g_b1g",
            "g_b2", "g_lng", "g_lnb", "g_qs", "g_ks", "g_qss", "g_kss",
            "g_qs_row", "g_ks_row"],
}
BLOCK_CONSTS = {
    "l0a": ["mst_l0"],
    "l0f": [],
    "l1a": [],
    "l1f": ["mst_l1", "hh_ind", "hh_bcast", "rotm", "t1q", "t2q", "t1k", "t2k"],
}

EMITTERS = {}


def build_program(blocks, dumps=()):
    nc = bass.Bass("TRN2", target_bir_lowering=False)
    xin = nc.dram_tensor("xin", [SEQS, NCH, 128, T], F32, kind="ExternalInput").ap()
    xout = nc.dram_tensor("xout", [SEQS, NCH, 128, T], F32, kind="ExternalOutput").ap()
    dump_aps = {n: nc.dram_tensor("dbg_" + n, shape, dt, kind="ExternalOutput").ap()
                for n, shape, dt in dumps}
    need_mid = len(blocks) > 1
    xh = nc.dram_tensor("xh", [SEQS, NCH, 128, T], F32).ap() if need_mid else None
    zh = (nc.dram_tensor("zh", [NCH, 128, T], F32).ap()
          if ("l1f" in blocks or "l1a" in blocks) else None)
    names = []
    wd = {}
    for b in blocks:
        names += BLOCK_WEIGHTS[b] + BLOCK_CONSTS[b]
    names = sorted(set(names))
    for n in names:
        if n in WEIGHT_SHAPES:
            wd[n] = nc.dram_tensor(n, WEIGHT_SHAPES[n], F32, kind="ExternalInput").ap()
        else:
            shape, dt = CONST_SHAPES[n]
            wd[n] = nc.dram_tensor(n, shape, dt, kind="ExternalInput").ap()
    scr = {}
    for n in ("f_w1", "f_w2", "f_h1", "f_h2", "f_w3", "a_win",
              "g_w1a", "g_w1g", "g_w2"):
        if n in names:
            scr[n] = nc.dram_tensor("scr_" + n, WEIGHT_SHAPES[n], BF16).ap()

    with contextlib.ExitStack() as stack:
        sc = Sched(nc, stack)
        pp = PsumPool(nc, stack)
        al = Alloc(nc, sc)
        mkp = lambda name, shape, dt=F32: R(stack.enter_context(
            nc.sbuf_tensor(name, shape, dt)))
        ctx = {"pp": pp, "sc": sc, "nc": nc,
               "wcache": set(), "wsbuf": {}, "wscr": scr}

        def dump(name, ap, reads=()):
            if name in dump_aps:
                sc.dma("sp", dump_aps[name], ap, reads=list(reads))
        ctx["dump"] = dump
        ctx["ones_col_bf"] = mkp("ones_col_bf", [128, 1], BF16)
        ctx["ones_row_bf"] = mkp("ones_row_bf", [1, 128], BF16)
        ctx["zeros_bf"] = mkp("zeros_bf", [128, TW], BF16)
        ctx["e5"] = mkp("e5", [1, 1], F32)
        ctx["e8"] = mkp("e8", [1, 1], F32)
        for nm, val in (("ones_col_bf", 1.0),
                        ("ones_row_bf", 1.0), ("zeros_bf", 0.0),
                        ("e5", 1e-5), ("e8", 1e-8)):
            r = ctx[nm]
            sc.op("pool", lambda e, r=r, val=val: e.memset(r.t[:], val), writes=[r])
        ctx["r_rec"] = mkp("r_rec", [1, TW], F32)
        _sq = [mkp(f"lnsq{j}", [128, TW], BF16) for j in range(2)]
        ctx["lnsets"] = [
            {"mu": mkp(f"lnmu{i}", [1, TW], F32),
             "tmp": mkp(f"lntmp{i}", [1, TW], F32),
             "rstd": mkp(f"lnrs{i}", [1, TW], F32),
             "sq": _sq}
            for i in range(2)]
        ctx["ln_i"] = 0
        ctx["bcsets"] = [(mkp(f"bch{i}", [1, TW], BF16),
                          mkp(f"bcl{i}", [1, TW], BF16)) for i in range(2)]
        ctx["bc_i"] = 0
        ctx["tmp_f"] = [mkp(f"tmpf{i}", [128, TW], F32) for i in range(4)]
        ctx["r_recs"] = [ctx["r_rec"]] + [mkp(f"r_rec{i}", [1, TW], F32)
                                          for i in range(1, 3)]
        ctx["xt"] = mkp("xt", [128, NCH * TW], F32)
        ctx["zh_buf"] = Buf()
        ctx["cast_stage"] = mkp("cast_stage", [128, NCH * TW], BF16)
        if zh is not None:
            wd["zh"] = zh
        for n in names:
            if n in CONST_SHAPES and n not in ("t1q", "t2q", "t1k", "t2k"):
                shape, dt = CONST_SHAPES[n]
                key = {"mst_l0": "mst0", "mst_l1": "mst1"}.get(n, n)
                ctx[key] = mkp("c_" + n, shape, dt)
                sc.dma("sp", ctx[key].t[:], wd[n], writes=[ctx[key]])
        sc.barrier()

        for s in range(SEQS):
            for i, b in enumerate(blocks):
                src = xin if i == 0 else xh
                dst = xout if i == len(blocks) - 1 else xh
                EMITTERS[b](sc, ctx, al, s, src, dst, wd)
                sc.barrier()
        sc.barrier()
        with nc.Block() as block:
            sc.replay(block)
    return nc, names


def shard_x(x_full):
    x = np.asarray(x_full, dtype=np.float32)
    shards = []
    for c in range(NCORES):
        xs = x[c * SEQS:(c + 1) * SEQS]
        xs = np.ascontiguousarray(xs.transpose(0, 2, 1))
        shards.append(xs.reshape(SEQS, NCH, 128, T))
    return shards


def unshard_x(outs):
    xs = []
    for o in outs:
        o = np.asarray(o, dtype=np.float32).reshape(SEQS, D, T)
        xs.append(o.transpose(0, 2, 1))
    return np.ascontiguousarray(np.concatenate(xs, axis=0))


def run_blocks(block_groups, inputs):
    consts = make_consts()
    wts = prep_weights(inputs)
    host = {**consts, **wts}
    cur = shard_x(inputs["L0_attn_x"])
    for blocks in block_groups:
        nc, names = build_program(blocks)
        in_maps = []
        for c in range(NCORES):
            m = {"xin": cur[c]}
            for n in names:
                m[n] = host[n]
            in_maps.append(m)
        res = run_bass_kernel_spmd(nc, in_maps, core_ids=list(range(NCORES)))
        cur = [np.asarray(res.results[c]["xout"]) for c in range(NCORES)]
    return unshard_x(cur)


def kernel(**inputs):
    out = run_blocks([("l0a", "l0f", "l1a", "l1f")], inputs)
    return out


EMITTERS["l0a"] = emit_l0a



def mm_group2(sc, p0, a0, p1, a1, items):
    n = len(items)
    readset = []
    tok = None
    for i, (l, r0, r1, reads) in enumerate(items):
        d = []
        if i == 0:
            d += p0.buf.wdep() + p1.buf.wdep()
        for x in reads:
            d += x.buf.rdep()
            readset.append(x)
        sc.emit("pe", lambda e, l=l, r0=r0, a0=a0, f=(i == 0), st=(i == n - 1):
                e.matmul(a0, l, r0, start=f, stop=st), deps=d, inc=0)
        tok = sc.emit("pe", lambda e, l=l, r1=r1, a1=a1, f=(i == 0),
                      st=(i == n - 1): e.matmul(a1, l, r1, start=f, stop=st),
                      deps=[], inc=1)
    for x in readset:
        x.buf.read(tok)
    p0.buf.wrote(tok)
    p1.buf.wrote(tok)
    return tok


def emit_l0f(sc, ctx, al, s, src_ap, dst_ap, wd):
    pp = ctx["pp"]
    TT2 = 2 * TW
    with al.scope() as mk:
        scl = mk("scl", 136)
        sc.dma("sp", scl.t[:, 0:32], wd["f_s1"], writes=[scl])
        sc.dma("sp", scl.t[:, 32:64], wd["f_s2"], writes=[scl])
        sc.dma("sp", scl.t[:, 64:96], wd["f_sh1"], writes=[scl])
        sc.dma("sp", scl.t[:, 96:128], wd["f_sh2"], writes=[scl])
        sc.dma("sp", scl.t[:, 128:136], wd["f_s3"], writes=[scl])
        g1 = mk("g1", 32 * TT2, BF16)

        for t2 in range(T // TT2):
            base = t2 * TT2
            scW = al.scope()
            mkW = scW.__enter__()
            xs2 = mkW("xs2", NCH * TT2, BF16)
            w12 = [mkW(f"w12_{i}", 8 * 128, BF16) for i in range(4)]
            for sub in range(2):
                t0 = base + sub * TW
                xt = load_xtile(sc, ctx, src_ap, s, t0)
                cs = cast_x_tile(sc, ctx, xt)

                def bf_fn(c, cs=cs):
                    return cs.t[:, c * TW:(c + 1) * TW], [cs]
                r_mu, r_rstd = ln_stats_tile(sc, ctx, bf_fn, want_mean=True)
                pmu = bcast_row(sc, ctx, r_mu)
                pa = bcast_row(sc, ctx, r_rstd)
                for c in range(NCH):
                    tf = ctx["tmp_f"][c % 2]
                    sc.op("dve", lambda e, c=c, tf=tf, xt=xt, pmu=pmu:
                          e.tensor_tensor(tf.t[:], xt.t[:, c * TW:(c + 1) * TW],
                                          pmu.t[:, 0:TW], ALU.subtract),
                          reads=[xt, pmu], writes=[tf])
                    sc.op("dve", lambda e, c=c, tf=tf, pa=pa, sub=sub, xs2=xs2:
                          e.tensor_tensor(
                              xs2.t[:, c * TT2 + sub * TW: c * TT2 + sub * TW + TW],
                              tf.t[:], pa.t[:, 0:TW], ALU.mult),
                          reads=[tf, pa], writes=[xs2])
            for oc in range(32):
                sa = w12[oc % 2]
                sb = w12[2 + oc % 2]
                cdma(sc, ctx, sa, sa.t[:], wd["f_w1"][oc],
                     ctx["wscr"]["f_w1"][oc], ("f_w1", oc))
                cdma(sc, ctx, sb, sb.t[:], wd["f_w2"][oc],
                     ctx["wscr"]["f_w2"][oc], ("f_w2", oc))
                pa0 = pp.take()
                pa1 = pp.take(exclude=[pa0])
                items = [(sa.t[:, kc * 128:(kc + 1) * 128],
                          xs2.t[:, kc * TT2: kc * TT2 + TW],
                          xs2.t[:, kc * TT2 + TW: kc * TT2 + TT2],
                          [sa, xs2]) for kc in range(NCH)]
                mm_group2(sc, pa0, pa0.t[:, 0:TW], pa1, pa1.t[:, 0:TW], items)
                pg0 = pp.take(exclude=[pa0, pa1])
                pg1 = pp.take(exclude=[pa0, pa1, pg0])
                items = [(sb.t[:, kc * 128:(kc + 1) * 128],
                          xs2.t[:, kc * TT2: kc * TT2 + TW],
                          xs2.t[:, kc * TT2 + TW: kc * TT2 + TT2],
                          [sb, xs2]) for kc in range(NCH)]
                mm_group2(sc, pg0, pg0.t[:, 0:TW], pg1, pg1.t[:, 0:TW], items)
                for sub, pa_, pg_ in ((0, pa0, pg0), (1, pa1, pg1)):
                    tf = ctx["tmp_f"][2 * (oc % 2) + sub]
                    sc.op("act", lambda e, pa_=pa_, tf=tf, oc=oc: e.activation(
                        tf.t[:], pa_.t[:, 0:TW], AF.Silu,
                        scale=scl.t[:, oc:oc + 1]), reads=[pa_, scl], writes=[tf])
                    sc.op("dve", lambda e, pg_=pg_, tf=tf, oc=oc, sub=sub:
                          e.scalar_tensor_tensor(
                              g1.t[:, oc * TT2 + sub * TW: oc * TT2 + sub * TW + TW],
                              pg_.t[:, 0:TW], scl.t[:, 32 + oc:33 + oc], tf.t[:],
                              ALU.mult, ALU.mult),
                          reads=[pg_, tf, scl], writes=[g1])
            scW.__exit__(None, None, None)

            scH = al.scope()
            mkH = scH.__enter__()
            g2 = mkH("g2", 32 * TT2, BF16)
            hstg = [mkH(f"hstg{i}", 32 * 128, BF16) for i in range(2)]
            for wname, scol, src_g, dst_g in (("f_h1", 64, g1, g2),
                                              ("f_h2", 96, g2, g1)):
                for oc in range(32):
                    st_ = hstg[oc % 2]
                    cdma(sc, ctx, st_, st_.t[:], wd[wname][oc],
                         ctx["wscr"][wname][oc], (wname, oc))
                    p0 = pp.take()
                    p1 = pp.take(exclude=[p0])
                    items = [(st_.t[:, kc * 128:(kc + 1) * 128],
                              src_g.t[:, kc * TT2: kc * TT2 + TW],
                              src_g.t[:, kc * TT2 + TW: kc * TT2 + TT2],
                              [st_, src_g]) for kc in range(32)]
                    mm_group2(sc, p0, p0.t[:, 0:TW], p1, p1.t[:, 0:TW], items)
                    for sub, p_ in ((0, p0), (1, p1)):
                        sc.op("act", lambda e, p_=p_, oc=oc, scol=scol, sub=sub,
                              dst_g=dst_g: e.activation(
                                  dst_g.t[:, oc * TT2 + sub * TW:
                                          oc * TT2 + sub * TW + TW],
                                  p_.t[:, 0:TW], AF.Gelu,
                                  scale=scl.t[:, scol + oc:scol + oc + 1]),
                              reads=[p_, scl], writes=[dst_g])
            for sub in range(2):
                t0 = base + sub * TW
                xt = load_xtile(sc, ctx, src_ap, s, t0)
                for oc in range(NCH):
                    st_ = hstg[oc % 2]
                    cdma(sc, ctx, st_, st_.t[:], wd["f_w3"][oc],
                         ctx["wscr"]["f_w3"][oc], ("f_w3", oc))
                    pb_ = pp.take()
                    items = [(pb_.t[:, 0:TW], st_.t[:, kc * 128:(kc + 1) * 128],
                              g1.t[:, kc * TT2 + sub * TW: kc * TT2 + sub * TW + TW],
                              [st_, g1]) for kc in range(32)]
                    sc.mm_group(pb_, items)
                    sc.op("dve", lambda e, pb_=pb_, oc=oc, xt=xt:
                          e.scalar_tensor_tensor(
                              xt.t[:, oc * TW:(oc + 1) * TW], pb_.t[:, 0:TW],
                              scl.t[:, 128 + oc:129 + oc],
                              xt.t[:, oc * TW:(oc + 1) * TW],
                              ALU.mult, ALU.add), reads=[pb_, scl], writes=[xt])
                store_xtile(sc, ctx, dst_ap, s, t0, xt)
            scH.__exit__(None, None, None)


EMITTERS["l0f"] = emit_l0f



def emit_l1a(sc, ctx, al, s, src_ap, dst_ap, wd):
    pp = ctx["pp"]
    with al.scope() as mk:
        ub = mk("ub", 2 * T, BF16)
        zb = mk("zb", 2 * T, BF16)
        sg = mk("sg", 2 * T)
        ut = mk("ut", 2 * T)
        dt_ = mk("dt_", 2 * T)
        hs = mk("hs", 2 * T)
        xacc_b = [mk(f"xacc{i}", NCH * TW) for i in range(2)]
        xst_b = [mk(f"xst{i}", NCH * TW, BF16) for i in range(2)]
        xnt_b = [ctx["cast_stage"], mk("xnt1", NCH * TW, BF16)]
        win = [mk(f"win{i}", 8 * 128, BF16) for i in range(2)]
        wsm = mk("wsm", 512 + 512 + 2048, BF16)
        bi = mk("bi", 26)
        for kc in range(2):
            sc.dma("pool", wsm.t[:, kc * 256:(kc + 1) * 256], wd["a_wx"][kc],
                   writes=[wsm])
            sc.dma("pool", wsm.t[:, 512 + kc * 256: 512 + (kc + 1) * 256],
                   wd["a_wdt"][kc], writes=[wsm])
            sc.dma("pool", wsm.t[:, 1024 + kc * 1024: 1024 + (kc + 1) * 1024],
                   wd["a_wout"][kc], writes=[wsm])
        sc.dma("sp", bi.t[:, 0:6], wd["a_bin"], writes=[bi])
        sc.dma("sp", bi.t[:, 6:8], wd["a_bx"], writes=[bi])
        sc.dma("sp", bi.t[:, 8:10], wd["a_bdt"], writes=[bi])
        sc.dma("sp", bi.t[:, 10:18], wd["a_bout"], writes=[bi])
        sc.dma("sp", bi.t[:, 18:26], wd["a_wnorm"], writes=[bi])
        ysb = ub

        for tt in range(NTT):
            t0 = tt * TW
            xacc = xacc_b[tt % 2]
            xst = xst_b[tt % 2]
            xnt = xnt_b[tt % 2]
            srcp = src_ap[s].rearrange("c p t -> p c t")[:, :, t0:t0 + TW]
            d = xacc.t[:, 0:NCH * TW].rearrange("p (c t) -> p c t", c=NCH)
            sc.dma("sp", d, srcp, writes=[xacc])
            for c in range(NCH):
                sc.op("dve", lambda e, c=c, xacc=xacc, xst=xst: e.tensor_copy(
                    xst.t[:, c * TW:(c + 1) * TW],
                    xacc.t[:, c * TW:(c + 1) * TW]),
                    reads=[xacc], writes=[xst])

            def bf_fn(c, xst=xst):
                return xst.t[:, c * TW:(c + 1) * TW], [xst]
            r_mu, r_rstd = ln_stats_tile(sc, ctx, bf_fn, want_mean=True)
            pmu = bcast_row(sc, ctx, r_mu)
            pa = bcast_row(sc, ctx, r_rstd)
            for c in range(NCH):
                tf = ctx["tmp_f"][c % 2]
                tf2 = ctx["tmp_f"][2 + c % 2]
                sc.op("dve", lambda e, c=c, tf=tf, xacc=xacc, pmu=pmu:
                      e.tensor_tensor(tf.t[:], xacc.t[:, c * TW:(c + 1) * TW],
                                      pmu.t[:, 0:TW], ALU.subtract),
                      reads=[xacc, pmu], writes=[tf])
                sc.op("dve", lambda e, tf=tf, tf2=tf2, pa=pa: e.tensor_tensor(
                    tf2.t[:], tf.t[:], pa.t[:, 0:TW], ALU.mult),
                    reads=[tf, pa], writes=[tf2])
                sc.op("dve", lambda e, c=c, tf2=tf2, xst=xst: e.tensor_copy(
                    xst.t[:, c * TW:(c + 1) * TW], tf2.t[:]),
                    reads=[tf2], writes=[xst])
                sc.op("dve", lambda e, c=c, tf2=tf2, xacc=xacc: e.tensor_tensor(
                    xacc.t[:, c * TW:(c + 1) * TW],
                    xacc.t[:, c * TW:(c + 1) * TW], tf2.t[:], ALU.add),
                    reads=[tf2], writes=[xacc])

            def bf2(c, xst=xst):
                return xst.t[:, c * TW:(c + 1) * TW], [xst]
            _, r_rstd2 = ln_stats_tile(sc, ctx, bf2, want_mean=False,
                                       eps_name="e8")
            pa2 = bcast_row(sc, ctx, r_rstd2)
            for c in range(NCH):
                sc.op("dve", lambda e, c=c, pa2=pa2, xst=xst, xnt=xnt:
                      e.scalar_tensor_tensor(
                          xnt.t[:, c * TW:(c + 1) * TW],
                          xst.t[:, c * TW:(c + 1) * TW],
                          bi.t[:, 18 + c:19 + c], pa2.t[:, 0:TW],
                          ALU.mult, ALU.mult),
                      reads=[xst, bi, pa2], writes=[xnt])
            for oc in range(6):
                wst_ = win[oc % 2]
                cdma(sc, ctx, wst_, wst_.t[:], wd["a_win"][oc],
                     ctx["wscr"]["a_win"][oc], ("a_win", oc))
                pb = pp.take()
                items = [(pb.t[:, 0:TW],
                          wst_.t[:, kc * 128:(kc + 1) * 128],
                          xnt.t[:, kc * TW:(kc + 1) * TW], [wst_, xnt])
                         for kc in range(NCH)]
                sc.mm_group(pb, items)
                if oc < 2:
                    dst, fn = ub, AF.Identity
                elif oc < 4:
                    dst, fn = zb, AF.Identity
                else:
                    dst, fn = sg, AF.Sigmoid
                co = oc % 2
                sc.op("act", lambda e, pb=pb, dst=dst, fn=fn, co=co, t0=t0, oc=oc:
                      e.activation(dst.t[:, co * T + t0:co * T + t0 + TW],
                                   pb.t[:, 0:TW], fn, bias=bi.t[:, oc:oc + 1],
                                   scale=1.0),
                      reads=[pb, bi], writes=[dst])
            for (srcb, wof, bof, fn, dst) in ((ub, 0, 6, AF.Tanh, ut),
                                              (zb, 512, 8, AF.Sigmoid, dt_)):
                for oc2 in range(2):
                    pb = pp.take()
                    items = [(pb.t[:, 0:TW],
                              wsm.t[:, wof + kc2 * 256 + oc2 * 128:
                                    wof + kc2 * 256 + oc2 * 128 + 128],
                              srcb.t[:, kc2 * T + t0: kc2 * T + t0 + TW],
                              [wsm, srcb]) for kc2 in range(2)]
                    sc.mm_group(pb, items)
                    sc.op("act", lambda e, pb=pb, fn=fn, dst=dst, oc2=oc2, t0=t0,
                          bof=bof: e.activation(
                              dst.t[:, oc2 * T + t0: oc2 * T + t0 + TW],
                              pb.t[:, 0:TW], fn,
                              bias=bi.t[:, bof + oc2:bof + oc2 + 1],
                              scale=1.0), reads=[pb, bi], writes=[dst])
            for c in range(2):
                lo = c * T + t0
                sc.op("dve", lambda e, lo=lo: e.tensor_tensor(
                    ut.t[:, lo:lo + TW], dt_.t[:, lo:lo + TW],
                    ut.t[:, lo:lo + TW], ALU.mult), reads=[dt_], writes=[ut])
                sc.op("dve", lambda e, lo=lo: e.tensor_scalar(
                    dt_.t[:, lo:lo + TW], dt_.t[:, lo:lo + TW],
                    -1.0, 1.0, ALU.mult, ALU.add), writes=[dt_])
                init = 0.0 if tt == 0 else hs.t[:, lo - 1:lo]
                sc.op("dve", lambda e, lo=lo, init=init: e.tensor_tensor_scan(
                    hs.t[:, lo:lo + TW], dt_.t[:, lo:lo + TW],
                    ut.t[:, lo:lo + TW], init, ALU.mult, ALU.add),
                    reads=[dt_, ut], writes=[hs])
                sc.op("dve", lambda e, lo=lo: e.tensor_tensor(
                    ysb.t[:, lo:lo + TW], hs.t[:, lo:lo + TW],
                    sg.t[:, lo:lo + TW], ALU.mult), reads=[hs, sg], writes=[ysb])
            for oc in range(NCH):
                pb = pp.take()
                items = [(pb.t[:, 0:TW],
                          wsm.t[:, 1024 + kc2 * 1024 + oc * 128:
                                1024 + kc2 * 1024 + oc * 128 + 128],
                          ysb.t[:, kc2 * T + t0: kc2 * T + t0 + TW],
                          [wsm, ysb]) for kc2 in range(2)]
                sc.mm_group(pb, items)
                sc.op("dve", lambda e, pb=pb, oc=oc, xacc=xacc:
                      e.scalar_tensor_tensor(
                          xacc.t[:, oc * TW:(oc + 1) * TW], pb.t[:, 0:TW],
                          bi.t[:, 10 + oc:11 + oc],
                          xacc.t[:, oc * TW:(oc + 1) * TW],
                          ALU.add, ALU.add), reads=[pb, bi], writes=[xacc])
            store_xtile(sc, ctx, dst_ap, s, t0, xacc)


EMITTERS["l1a"] = emit_l1a



def store_ztile(sc, ctx, zh_ap, zh_buf, t0, src_r):
    dst = zh_ap.rearrange("c p t -> p c t")[:, :, t0:t0 + TW]
    sap = src_r.t[:, 0:NCH * TW].rearrange("p (c t) -> p c t", c=NCH)
    sc.dma("sp", dst, sap, reads=[src_r], extra=zh_buf.wdep())
    tok = src_r.buf.rs[-1]
    zh_buf.wrote(tok)


def load_ztile(sc, ctx, zh_ap, zh_buf, t0, dst):
    srcp = zh_ap.rearrange("c p t -> p c t")[:, :, t0:t0 + TW]
    d = dst.t[:, 0:NCH * TW].rearrange("p (c t) -> p c t", c=NCH)
    sc.dma("sp", d, srcp, writes=[dst], extra=zh_buf.rdep())
    zh_buf.read(dst.buf.w)


def load_zhalf(sc, ctx, zh_ap, zh_buf, base, width, dst):
    srcp = zh_ap.rearrange("c p t -> p c t")[:, :, base:base + width]
    d = dst.t[:, 0:NCH * width].rearrange("p (c t) -> p c t", c=NCH)
    sc.dma("sp", d, srcp, writes=[dst], extra=zh_buf.rdep())
    zh_buf.read(dst.buf.w)


def store_zhalf(sc, ctx, zh_ap, zh_buf, base, width, src_r):
    dst = zh_ap.rearrange("c p t -> p c t")[:, :, base:base + width]
    sap = src_r.t[:, 0:NCH * width].rearrange("p (c t) -> p c t", c=NCH)
    sc.dma("sp", dst, sap, reads=[src_r], extra=zh_buf.wdep())
    zh_buf.wrote(src_r.buf.rs[-1])


def emit_shift_cast_half(sc, ctx, zh_ap, zh_buf, base, width, dst):
    for c in range(NCH):
        col = c * width
        if c < 4:
            tok = sc.dma("pool", dst.t[:, col:col + width],
                         zh_ap[c][:, base:base + width], writes=[dst],
                         extra=zh_buf.rdep())
        elif base == 0:
            tok = sc.dma("pool", dst.t[:, col + 1:col + width],
                         zh_ap[c][:, 0:width - 1], writes=[dst],
                         extra=zh_buf.rdep())
            sc.op("dve", lambda e, col=col, dst=dst: e.memset(
                dst.t[:, col:col + 1], 0.0), writes=[dst])
        else:
            tok = sc.dma("pool", dst.t[:, col:col + width],
                         zh_ap[c][:, base - 1:base + width - 1], writes=[dst],
                         extra=zh_buf.rdep())
        zh_buf.read(tok)


def emit_shift_cast(sc, ctx, zh_ap, zh_buf, dst):
    for c in range(NCH):
        if c < 4:
            tok = sc.dma("pool", dst.t[:, c * T:(c + 1) * T], zh_ap[c],
                         writes=[dst], extra=zh_buf.rdep())
        else:
            tok = sc.dma("pool", dst.t[:, c * T + 1:(c + 1) * T],
                         zh_ap[c][:, 0:T - 1], writes=[dst],
                         extra=zh_buf.rdep())
            sc.op("dve", lambda e, c=c, dst=dst: e.memset(
                dst.t[:, c * T:c * T + 1], 0.0), writes=[dst])
        zh_buf.read(tok)


def emit_l1f(sc, ctx, al, s, src_ap, dst_ap, wd):
    pp = ctx["pp"]
    HB = 2816 // 128
    zh = wd["zh"]
    zh_buf = ctx["zh_buf"]
    top = al.scope()
    mkt = top.__enter__()
    zt = mkt("zt", NCH * TW)

    for tt in range(NTT):
        t0 = tt * TW
        xt = load_xtile(sc, ctx, src_ap, s, t0)
        cs = cast_x_tile(sc, ctx, xt)

        def bf_fn(c, cs=cs):
            return cs.t[:, c * TW:(c + 1) * TW], [cs]
        r_mu, r_rstd = ln_stats_tile(sc, ctx, bf_fn, want_mean=True)
        pmu = bcast_row(sc, ctx, r_mu)
        pa = bcast_row(sc, ctx, r_rstd)
        for c in range(NCH):
            tf = ctx["tmp_f"][c % 2]
            sc.op("dve", lambda e, c=c, tf=tf, xt=xt, pmu=pmu: e.tensor_tensor(
                tf.t[:], xt.t[:, c * TW:(c + 1) * TW], pmu.t[:, 0:TW],
                ALU.subtract), reads=[xt, pmu], writes=[tf])
            sc.op("dve", lambda e, c=c, tf=tf, pa=pa, zt=zt: e.tensor_tensor(
                zt.t[:, c * TW:(c + 1) * TW], tf.t[:], pa.t[:, 0:TW],
                ALU.mult), reads=[tf, pa], writes=[zt])
        store_ztile(sc, ctx, zh, zh_buf, t0, zt)

    for l in range(2):
        with al.scope() as mk2:
            qf = mk2("qf", 4 * T, BF16)
            kf = mk2("kf", 4 * T, BF16)
            vt = mk2("vt", 16 * 520, BF16)
            ao = mk2("ao", 4 * T, BF16)
            negc = mk2("negc", 1)
            scA1 = al.scope()
            mkA1 = scA1.__enter__()
            hshb = mkA1("hshb", NCH * T, BF16)
            qkw = [mkA1(f"qkw{i}", 8 * 128, BF16) for i in range(2)]
            qbt_b = [mkA1(f"qbt{i}", TW, BF16) for i in range(2)]
            sqb_b = [mkA1(f"sqb{i}", TW, BF16) for i in range(2)]
            fhi_b = [mkA1(f"fhi{i}", TW, BF16, parts=2) for i in range(2)]
            flo_b = [mkA1(f"flo{i}", TW, BF16, parts=2) for i in range(2)]
            frow_b = [mkA1(f"frow{i}", TW, parts=2) for i in range(2)]
            scp = mkA1("scp", 4)
            qrow = mkA1("qrow", 64, parts=1)
            krow = mkA1("krow", 64, parts=1)
            sc.dma("sp", scp.t[:, 0:1], wd["g_qs"][l], writes=[scp])
            sc.dma("sp", scp.t[:, 1:2], wd["g_qss"][l], writes=[scp])
            sc.dma("sp", scp.t[:, 2:3], wd["g_ks"][l], writes=[scp])
            sc.dma("sp", scp.t[:, 3:4], wd["g_kss"][l], writes=[scp])
            sc.dma("sp", qrow.t[:], wd["g_qs_row"][l], writes=[qrow])
            sc.dma("sp", krow.t[:], wd["g_ks_row"][l], writes=[krow])

            r_rec = ctx["r_rec"]
            sc.op("dve", lambda e: e.reduce_max(
                r_rec.t[0:1, 0:1], qrow.t[:], axis=AX.X,
                apply_absolute_value=True), reads=[qrow], writes=[r_rec])
            sc.op("dve", lambda e: e.reduce_max(
                r_rec.t[0:1, 1:2], krow.t[:], axis=AX.X,
                apply_absolute_value=True), reads=[krow], writes=[r_rec])
            sc.op("dve", lambda e: e.tensor_tensor(
                r_rec.t[0:1, 0:1], r_rec.t[0:1, 0:1], r_rec.t[0:1, 1:2],
                ALU.mult), writes=[r_rec])
            sc.op("dve", lambda e: e.tensor_scalar(
                r_rec.t[0:1, 0:1], r_rec.t[0:1, 0:1], -8.0, 40.0,
                ALU.mult, ALU.add), writes=[r_rec])
            sc.op("dve", lambda e: e.tensor_scalar_min(
                r_rec.t[0:1, 0:1], r_rec.t[0:1, 0:1], 0.0), writes=[r_rec])
            pbn = bcast_row(sc, ctx, r_rec, row_ap=r_rec.t[0:1, 0:1], width=1)
            sc.op("dve", lambda e, pbn=pbn: e.tensor_copy(
                negc.t[:], pbn.t[:, 0:1]), reads=[pbn], writes=[negc])

            emit_shift_cast(sc, ctx, zh, zh_buf, hshb)

            scTab = al.scope()
            mkTab = scTab.__enter__()
            tabs = mkTab("tabs", 2 * T)
            for part, ql in (("q", True), ("k", False)):
                t1n = "t1q" if ql else "t1k"
                t2n = "t2q" if ql else "t2k"
                sc.dma("sp", tabs.t[:, 0:T], wd[t1n], writes=[tabs])
                sc.dma("sp", tabs.t[:, T:2 * T], wd[t2n], writes=[tabs])
                sci = 0 if ql else 2
                for half, col in ((slice(0, T), sci), (slice(T, 2 * T), sci + 1)):
                    sc.op("dve", lambda e, half=half, col=col: e.tensor_scalar(
                        tabs.t[:, half], tabs.t[:, half], scp.t[:, col:col + 1],
                        None, ALU.mult), reads=[scp], writes=[tabs])
                dstf = qf if ql else kf
                for oc4 in range(4):
                    oc = oc4 if ql else 4 + oc4
                    st_ = qkw[oc4 % 2]
                    sc.dma("pool", st_.t[:], wd["g_wqkv_qk"][l][oc], writes=[st_])
                    for tt in range(NTT):
                        t0 = tt * TW
                        u = (oc4 * NTT + tt) % 2
                        qbt = qbt_b[u]
                        sqb = sqb_b[u]
                        fhi = fhi_b[u]
                        flo = flo_b[u]
                        fr = frow_b[u]
                        pb = pp.take()
                        items = [(pb.t[:, 0:TW],
                                  st_.t[:, kc * 128:(kc + 1) * 128],
                                  hshb.t[:, kc * T + t0: kc * T + t0 + TW],
                                  [st_, hshb]) for kc in range(NCH)]
                        sc.mm_group(pb, items)
                        sc.op("act", lambda e, pb=pb, qbt=qbt: e.activation(
                            qbt.t[:], pb.t[:, 0:TW], AF.Identity, bias=0.0,
                            scale=1.0), reads=[pb], writes=[qbt])
                        sc.op("act", lambda e, pb=pb, sqb=sqb: e.activation(
                            sqb.t[:], pb.t[:, 0:TW], AF.Square),
                            reads=[pb], writes=[sqb])
                        ph = pp.take(exclude=[pb])
                        sc.mm_group(ph, [(ph.t[0:2, 0:TW], ctx["hh_ind"].t[:],
                                          sqb.t[:], [sqb, ctx["hh_ind"]])])
                        sc.op("act", lambda e, ph=ph, fr=fr: e.activation(
                            fr.t[0:2, :], ph.t[0:2, 0:TW], AF.Ln),
                            reads=[ph], writes=[fr])
                        sc.op("act", lambda e, fr=fr: e.activation(
                            fr.t[0:2, :], fr.t[0:2, :], AF.Exp, scale=-0.5),
                            writes=[fr])
                        sc.op("dve", lambda e, fr=fr, ql=ql: e.tensor_scalar(
                            fr.t[0:2, :], fr.t[0:2, :], 1e12, 8.0 if ql else 1.0,
                            ALU.min, ALU.mult), writes=[fr])
                        sc.op("dve", lambda e, fr=fr, fhi=fhi: e.tensor_copy(
                            fhi.t[:], fr.t[0:2, :]), reads=[fr], writes=[fhi])
                        sc.op("dve", lambda e, fr=fr, fhi=fhi, flo=flo:
                              e.tensor_tensor(flo.t[:], fr.t[0:2, :], fhi.t[:],
                                              ALU.subtract),
                              reads=[fr, fhi], writes=[flo])
                        pf = pp.take(exclude=[pb, ph])
                        sc.mm_group(pf, [
                            (pf.t[:, 0:TW], ctx["hh_bcast"].t[:], fhi.t[:],
                             [fhi, ctx["hh_bcast"]]),
                            (pf.t[:, 0:TW], ctx["hh_bcast"].t[:], flo.t[:],
                             [flo])])
                        pr = pp.take(exclude=[pb, ph, pf])
                        sc.mm_group(pr, [(pr.t[:, 0:TW], ctx["rotm"].t[:],
                                          qbt.t[:], [qbt, ctx["rotm"]])])
                        tf0 = ctx["tmp_f"][2 * (u % 2)]
                        tf1 = ctx["tmp_f"][2 * (u % 2) + 1]
                        sc.op("dve", lambda e, tf0=tf0, t0=t0, qbt=qbt:
                              e.tensor_tensor(tf0.t[:], qbt.t[:],
                                              tabs.t[:, t0:t0 + TW], ALU.mult),
                              reads=[qbt, tabs], writes=[tf0])
                        sc.op("dve", lambda e, tf1=tf1, pr=pr, t0=t0:
                              e.tensor_tensor(tf1.t[:], pr.t[:, 0:TW],
                                              tabs.t[:, T + t0:T + t0 + TW],
                                              ALU.mult),
                              reads=[pr, tabs], writes=[tf1])
                        sc.op("dve", lambda e, tf0=tf0, tf1=tf1: e.tensor_tensor(
                            tf0.t[:], tf0.t[:], tf1.t[:], ALU.add),
                            reads=[tf1], writes=[tf0])
                        sc.op("dve", lambda e, tf0=tf0, pf=pf, dstf=dstf,
                              oc4=oc4, t0=t0: e.tensor_tensor(
                                  dstf.t[:, oc4 * T + t0: oc4 * T + t0 + TW],
                                  tf0.t[:], pf.t[:, 0:TW], ALU.mult),
                              reads=[tf0, pf], writes=[dstf])

            scTab.__exit__(None, None, None)

            vws = mkA1("vws", 8 * 512, BF16)
            for kc in range(NCH):
                sc.dma("pool", vws.t[:, kc * 512:(kc + 1) * 512],
                       wd["g_wqkv_v"][l][kc], writes=[vws])
            onesap = vt.t[:, 0:16 * 520].rearrange(
                "p (tc h o) -> p tc h o", tc=16, h=8)[:, :, :, 64:65]
            sc.op("dve", lambda e, onesap=onesap: e.memset(onesap, 1.0),
                  writes=[vt])
            for tc in range(16):
                pb = pp.take()
                items = [(pb.t[:, 0:512],
                          hshb.t[:, kc * T + tc * 128: kc * T + tc * 128 + 128],
                          vws.t[:, kc * 512:(kc + 1) * 512],
                          [vws, hshb]) for kc in range(NCH)]
                sc.mm_group(pb, items)
                d = vt.t[:, tc * 520: tc * 520 + 520].rearrange(
                    "p (h o) -> p h o", h=8)[:, :, 0:64]
                srcap = pb.t[:, 0:512].rearrange("p (h o) -> p h o", h=8)
                sc.op("dve", lambda e, d=d, srcap=srcap: e.tensor_copy(d, srcap),
                      reads=[pb], writes=[vt])

            scA1.__exit__(None, None, None)
            e_b = [mk2(f"el{i}", TW, BF16) for i in range(3)]
            em_b = [mk2(f"em{i}", TW, BF16) for i in range(3)]

            for h in range(8):
                rbase = (h % 2) * 64
                kcc = h // 2
                for w in range(4):
                    pav = pp.take()
                    kjs = [j for j in range(8) if (w - 1) * 4 + j >= 0]
                    nkj = len(kjs)
                    for b0 in range(0, nkj, 3):
                        batch = list(enumerate(kjs))[b0:b0 + 3]
                        stiles = []
                        for ji, j in batch:
                            tc = (w - 1) * 4 + j
                            pbs = pp.take(exclude=[pav])
                            kap = kf.t[rbase:rbase + 64,
                                       kcc * T + tc * 128: kcc * T + tc * 128 + 128]
                            qap = qf.t[rbase:rbase + 64,
                                       kcc * T + w * TW: kcc * T + (w + 1) * TW]
                            sc.mm_group(pbs, [(pbs.t[:, 0:TW], kap, qap, [kf, qf])])
                            stiles.append((ji, j, pbs))
                        ems = []
                        for ji, j, pbs in stiles:
                            eb = e_b[ji % 3]
                            sc.op("act", lambda e, pbs=pbs, eb=eb: e.activation(
                                eb.t[:], pbs.t[:, 0:TW], AF.Exp, bias=negc.t[:],
                                scale=1.0), reads=[pbs, negc], writes=[eb])
                            off = 896 - 128 * j
                            em = em_b[ji % 3]
                            sc.op("dve", lambda e, eb=eb, off=off, em=em: e.select(
                                em.t[:], ctx["mst1"].t[:, off:off + TW], eb.t[:],
                                ctx["zeros_bf"].t[:]),
                                reads=[eb, ctx["mst1"], ctx["zeros_bf"]],
                                writes=[em])
                            ems.append((ji, j, em))
                        for ji, j, em in ems:
                            tc = (w - 1) * 4 + j
                            vap = vt.t[:, tc * 520 + h * 65: tc * 520 + h * 65 + 65]
                            d = [] if ji > 0 else list(pav.buf.wdep())
                            d = d + em.buf.rdep() + vt.buf.rdep()
                            tok = sc.emit("pe", lambda e, vap=vap, em=em, pav=pav,
                                          f=(ji == 0), st=(ji == nkj - 1): e.matmul(
                                              pav.t[0:65, 0:TW], vap, em.t[:],
                                              start=f, stop=st),
                                          deps=d, inc=1)
                            em.buf.read(tok)
                            vt.buf.read(tok)
                            if ji == nkj - 1:
                                pav.buf.wrote(tok)
                    r_rec2 = ctx["r_recs"][(h * 4 + w) % 3]
                    sc.op("act", lambda e, pav=pav, r_rec2=r_rec2: e.activation(
                        r_rec2.t[:], pav.t[64:65, 0:TW], AF.Ln),
                        reads=[pav], writes=[r_rec2])
                    sc.op("act", lambda e, r_rec2=r_rec2: e.activation(
                        r_rec2.t[:], r_rec2.t[:], AF.Exp, scale=-1.0),
                        writes=[r_rec2])
                    pbc = bcast_row(sc, ctx, r_rec2)
                    tf0 = ctx["tmp_f"][(h * 4 + w) % 4]
                    sc.op("dve", lambda e, pav=pav, tf0=tf0: e.tensor_copy(
                        tf0.t[0:64, :], pav.t[0:64, 0:TW]),
                        reads=[pav], writes=[tf0])
                    aod = ao.t[rbase:rbase + 64,
                               kcc * T + w * TW: kcc * T + (w + 1) * TW]
                    sc.op("dve", lambda e, tf0=tf0, aod=aod, pbc=pbc:
                          e.tensor_tensor(aod, tf0.t[0:64, :],
                                          pbc.t[0:64, 0:TW], ALU.mult),
                          reads=[tf0, pbc], writes=[ao])

            wom = [mk2(f"wom{i}", 4 * 128, BF16) for i in range(2)]
            for tt in range(NTT):
                t0 = tt * TW
                load_ztile(sc, ctx, zh, zh_buf, t0, zt)
                for oc in range(NCH):
                    ws_ = wom[oc % 2]
                    sc.dma("pool", ws_.t[:], wd["g_wo"][l][oc], writes=[ws_])
                    pb = pp.take()
                    items = [(pb.t[:, 0:TW],
                              ws_.t[:, kc * 128:(kc + 1) * 128],
                              ao.t[:, kc * T + t0: kc * T + t0 + TW],
                              [ws_, ao]) for kc in range(4)]
                    sc.mm_group(pb, items)
                    sc.op("dve", lambda e, pb=pb, oc=oc, zt=zt:
                          e.scalar_tensor_tensor(
                              zt.t[:, oc * TW:(oc + 1) * TW], pb.t[:, 0:TW],
                              1.0, zt.t[:, oc * TW:(oc + 1) * TW],
                              ALU.mult, ALU.add), reads=[pb], writes=[zt])
                store_ztile(sc, ctx, zh, zh_buf, t0, zt)

        TT2 = 2 * TW
        with al.scope() as mk3:
            gb = mk3("gb", 26)
            b1s = mk3("b1s", 44)
            sc.dma("sp", gb.t[:, 0:8], wd["g_lng"][l], writes=[gb])
            sc.dma("sp", gb.t[:, 8:16], wd["g_lnb"][l], writes=[gb])
            sc.dma("sp", gb.t[:, 16:24], wd["g_b2"][l], writes=[gb])
            sc.dma("sp", b1s.t[:, 0:22], wd["g_b1a"][l], writes=[b1s])
            sc.dma("sp", b1s.t[:, 22:44], wd["g_b1g"][l], writes=[b1s])
            for t2 in range(T // TT2):
                base = t2 * TT2
                scX = al.scope()
                mkX = scX.__enter__()
                xs22 = mkX("xs22", NCH * TT2, BF16)
                scS = al.scope()
                mkS = scS.__enter__()
                hsh2 = mkS("hsh2", NCH * TT2, BF16)
                emit_shift_cast_half(sc, ctx, zh, zh_buf, base, TT2, hsh2)
                for sub in range(2):
                    t0r = sub * TW

                    def bfh(c, t0r=t0r, hsh2=hsh2):
                        return hsh2.t[:, c * TT2 + t0r:c * TT2 + t0r + TW], [hsh2]
                    r_mu, r_rstd = ln_stats_tile(sc, ctx, bfh, want_mean=True)
                    pmu = bcast_row(sc, ctx, r_mu)
                    pa = bcast_row(sc, ctx, r_rstd)
                    for c in range(NCH):
                        tf0, tf1 = ctx["tmp_f"][0], ctx["tmp_f"][1]
                        sc.op("dve", lambda e, c=c, tf0=tf0, t0r=t0r, pmu=pmu,
                              hsh2=hsh2: e.tensor_tensor(
                                  tf0.t[:],
                                  hsh2.t[:, c * TT2 + t0r:c * TT2 + t0r + TW],
                                  pmu.t[:, 0:TW], ALU.subtract),
                              reads=[hsh2, pmu], writes=[tf0])
                        sc.op("dve", lambda e, c=c, tf0=tf0, tf1=tf1, pa=pa:
                              e.scalar_tensor_tensor(tf1.t[:], tf0.t[:],
                                                     gb.t[:, c:c + 1],
                                                     pa.t[:, 0:TW],
                                                     ALU.mult, ALU.mult),
                              reads=[tf0, gb, pa], writes=[tf1])
                        sc.op("dve", lambda e, c=c, tf1=tf1, t0r=t0r:
                              e.tensor_scalar(
                                  xs22.t[:, c * TT2 + t0r:c * TT2 + t0r + TW],
                                  tf1.t[:], 1.0, gb.t[:, 8 + c:9 + c],
                                  ALU.mult, ALU.add),
                              reads=[tf1, gb], writes=[xs22])
                scS.__exit__(None, None, None)
                hb2 = mkX("hb2", HB * TT2, BF16)
                w1st = [mkX(f"w1st{i}", 8 * 128, BF16) for i in range(4)]
                for oc in range(HB):
                    sa = w1st[oc % 2]
                    sg_ = w1st[2 + oc % 2]
                    cdma(sc, ctx, sa, sa.t[:], wd["g_w1a"][l][oc],
                         ctx["wscr"]["g_w1a"][l][oc], ("g_w1a", l, oc))
                    cdma(sc, ctx, sg_, sg_.t[:], wd["g_w1g"][l][oc],
                         ctx["wscr"]["g_w1g"][l][oc], ("g_w1g", l, oc))
                    pa0 = pp.take()
                    pa1 = pp.take(exclude=[pa0])
                    items = [(sa.t[:, kc * 128:(kc + 1) * 128],
                              xs22.t[:, kc * TT2: kc * TT2 + TW],
                              xs22.t[:, kc * TT2 + TW: kc * TT2 + TT2],
                              [sa, xs22]) for kc in range(NCH)]
                    mm_group2(sc, pa0, pa0.t[:, 0:TW], pa1, pa1.t[:, 0:TW], items)
                    pg0 = pp.take(exclude=[pa0, pa1])
                    pg1 = pp.take(exclude=[pa0, pa1, pg0])
                    items = [(sg_.t[:, kc * 128:(kc + 1) * 128],
                              xs22.t[:, kc * TT2: kc * TT2 + TW],
                              xs22.t[:, kc * TT2 + TW: kc * TT2 + TT2],
                              [sg_, xs22]) for kc in range(NCH)]
                    mm_group2(sc, pg0, pg0.t[:, 0:TW], pg1, pg1.t[:, 0:TW], items)
                    for sub, pa_, pg_ in ((0, pa0, pg0), (1, pa1, pg1)):
                        tf0 = ctx["tmp_f"][2 * (oc % 2) + sub]
                        sc.op("act", lambda e, pg_=pg_, tf0=tf0, oc=oc:
                              e.activation(tf0.t[:], pg_.t[:, 0:TW], AF.Gelu,
                                           bias=b1s.t[:, 22 + oc:23 + oc],
                                           scale=1.0),
                              reads=[pg_, b1s], writes=[tf0])
                        sc.op("dve", lambda e, pa_=pa_, tf0=tf0, oc=oc, sub=sub:
                              e.scalar_tensor_tensor(
                                  hb2.t[:, oc * TT2 + sub * TW:
                                        oc * TT2 + sub * TW + TW],
                                  pa_.t[:, 0:TW], b1s.t[:, oc:oc + 1], tf0.t[:],
                                  ALU.add, ALU.mult),
                              reads=[pa_, b1s, tf0], writes=[hb2])
                ztt = mkX("ztt", NCH * TT2)
                w2st = [mkX(f"w2st{i}", HB * 128, BF16) for i in range(2)]
                load_zhalf(sc, ctx, zh, zh_buf, base, TT2, ztt)
                for oc in range(NCH):
                    st_ = w2st[oc % 2]
                    cdma(sc, ctx, st_, st_.t[:], wd["g_w2"][l][oc],
                         ctx["wscr"]["g_w2"][l][oc], ("g_w2", l, oc))
                    p0 = pp.take()
                    p1 = pp.take(exclude=[p0])
                    items = [(st_.t[:, kc * 128:(kc + 1) * 128],
                              hb2.t[:, kc * TT2: kc * TT2 + TW],
                              hb2.t[:, kc * TT2 + TW: kc * TT2 + TT2],
                              [st_, hb2]) for kc in range(HB)]
                    mm_group2(sc, p0, p0.t[:, 0:TW], p1, p1.t[:, 0:TW], items)
                    for sub, p_ in ((0, p0), (1, p1)):
                        tf1 = ctx["tmp_f"][2 * (oc % 2) + sub]
                        sc.op("dve", lambda e, p_=p_, oc=oc, tf1=tf1:
                              e.tensor_scalar(
                                  tf1.t[:], p_.t[:, 0:TW], 1.0,
                                  gb.t[:, 16 + oc:17 + oc],
                                  ALU.mult, ALU.add), reads=[p_, gb], writes=[tf1])
                        sc.op("dve", lambda e, oc=oc, sub=sub, tf1=tf1, ztt=ztt:
                              e.tensor_tensor(
                                  ztt.t[:, oc * TT2 + sub * TW:
                                        oc * TT2 + sub * TW + TW],
                                  ztt.t[:, oc * TT2 + sub * TW:
                                        oc * TT2 + sub * TW + TW],
                                  tf1.t[:], ALU.add), reads=[tf1], writes=[ztt])
                store_zhalf(sc, ctx, zh, zh_buf, base, TT2, ztt)
                scX.__exit__(None, None, None)

    for tt in range(NTT):
        t0 = tt * TW
        xt = load_xtile(sc, ctx, src_ap, s, t0)
        load_ztile(sc, ctx, zh, zh_buf, t0, zt)
        for c in range(NCH):
            sc.op("dve", lambda e, c=c, xt=xt, zt=zt: e.tensor_tensor(
                xt.t[:, c * TW:(c + 1) * TW], xt.t[:, c * TW:(c + 1) * TW],
                zt.t[:, c * TW:(c + 1) * TW], ALU.add),
                reads=[zt], writes=[xt])
        store_xtile(sc, ctx, dst_ap, s, t0, xt)
    top.__exit__(None, None, None)


EMITTERS["l1f"] = emit_l1f


# version 21
# speedup vs baseline: 1.0117x; 1.0117x over previous
import contextlib
import numpy as np
import ml_dtypes
import concourse.bass as bass
import concourse.mybir as mybir
import concourse.bass_utils as _bass_utils
from concourse.bass_utils import run_bass_kernel_spmd

if not getattr(_bass_utils, "_ldwopt_patched", False):
    _ORIG_RUN_COMMAND = _bass_utils.run_command

    def _run_command_ldwopt(argv, **kwargs):
        argv = ["--enable-ldw-opt=true" if a == "--enable-ldw-opt=false" else a
                for a in argv]
        return _ORIG_RUN_COMMAND(argv, **kwargs)

    _bass_utils.run_command = _run_command_ldwopt
    _bass_utils._ldwopt_patched = True

F32 = mybir.dt.float32
BF16 = mybir.dt.bfloat16
U8 = mybir.dt.uint8
AX = mybir.AxisListType
ALU = mybir.AluOpType
AF = mybir.ActivationFunctionType
BF = ml_dtypes.bfloat16

NCORES = 8
SEQS = 2
T = 2048
D = 1024
NCH = 8
TW = 512
NTT = T // TW

ALL_BLOCKS = ("l0a", "l0f", "l1a", "l1f")


QUEUES = ("pe", "act", "dve", "pool", "sp")


class Buf:
    __slots__ = ("w", "rs")

    def __init__(self):
        self.w = None
        self.rs = []

    def rdep(self):
        return [self.w] if self.w else []

    def wdep(self):
        return ([self.w] if self.w else []) + self.rs

    def wrote(self, tok):
        assert tok is not None
        self.w = tok
        self.rs = []

    def read(self, tok):
        assert tok is not None
        self.rs.append(tok)


class R:
    __slots__ = ("t", "buf")

    def __init__(self, t):
        self.t = t
        self.buf = Buf()


class Sched:

    def __init__(self, nc, stack):
        self.nc = nc
        self.q = {e: [] for e in QUEUES}
        names = ("pe", "act", "dve", "pool", "sp", "dsp", "dpool", "dact")
        self.sem = {k: stack.enter_context(nc.semaphore(f"s_{k}")) for k in names}
        self.cnt = {k: 0 for k in names}
        self.waited = {e: {} for e in QUEUES}
        self.dma_issued = {"sp": 0, "pool": 0, "act": 0}
        self.dma_flushed = {"sp": 0, "pool": 0, "act": 0}
        self.dma_flush_tok = {"sp": None, "pool": None, "act": None}

    def _dsem(self, qname):
        return {"sp": "dsp", "pool": "dpool", "act": "dact"}[qname]

    def _flush_dma(self, qname, upto):
        if self.dma_flushed[qname] >= upto:
            return self.dma_flush_tok[qname]
        dsem = self._dsem(qname)
        issued = self.dma_issued[qname]
        q = self.q[qname]
        want = 16 * issued
        if want > self.waited[qname].get(dsem, 0):
            q.append(("w", dsem, want))
            self.waited[qname][dsem] = want
        own = qname
        self.cnt[own] += 1
        sem = self.sem[own]
        q.append(("d", lambda e, sem=sem: e.sem_inc(sem, 1)))
        tok = (own, self.cnt[own])
        self.dma_flushed[qname] = issued
        self.dma_flush_tok[qname] = tok
        return tok

    def _resolve(self, d):
        if d is None or d[0] != "dma":
            return d
        _, qname, idx = d
        return self._flush_dma(qname, idx)

    def emit(self, eng, fn, deps=(), sem_name=None, inc=1):
        q = self.q[eng]
        for d in deps:
            d = self._resolve(d)
            if d is None:
                continue
            s, v = d
            if s == eng:
                continue
            if v > self.waited[eng].get(s, 0):
                q.append(("w", s, v))
                self.waited[eng][s] = v
        if inc:
            s = sem_name or eng
            self.cnt[s] += inc
            q.append(("i", fn, s, inc))
            return (s, self.cnt[s])
        q.append(("d", fn))
        return None

    def op(self, eng, fn, reads=(), writes=(), extra=(), sem_name=None):
        deps = list(extra)
        for r in reads:
            deps += r.buf.rdep()
        for r in writes:
            deps += r.buf.wdep()
        tok = self.emit(eng, fn, deps, sem_name, 1)
        for r in reads:
            r.buf.read(tok)
        for r in writes:
            r.buf.wrote(tok)
        return tok

    def dma(self, qname, out, in_, reads=(), writes=(), extra=()):
        deps = list(extra)
        for r in reads:
            deps += r.buf.rdep()
        for r in writes:
            deps += r.buf.wdep()
        dsem = self._dsem(qname)
        q = self.q[qname]
        for d in deps:
            d = self._resolve(d)
            if d is None:
                continue
            s, v = d
            if s == qname:
                continue
            if v > self.waited[qname].get(s, 0):
                q.append(("w", s, v))
                self.waited[qname][s] = v
        self.cnt[dsem] += 16
        q.append(("i", lambda e, out=out, in_=in_: e.dma_start(out=out, in_=in_),
                  dsem, 16))
        self.dma_issued[qname] += 1
        tok = ("dma", qname, self.dma_issued[qname])
        for r in reads:
            r.buf.read(tok)
        for r in writes:
            r.buf.wrote(tok)
        return tok

    def mm_group(self, psum_r, items, extra=()):
        n = len(items)
        readset = []
        tok = None
        for i, (pap, l, rr, reads) in enumerate(items):
            d = []
            if i == 0:
                d += list(extra) + psum_r.buf.wdep()
            for x in reads:
                d += x.buf.rdep()
                readset.append(x)
            tok = self.emit(
                "pe",
                lambda e, pap=pap, l=l, rr=rr, f=(i == 0), s=(i == n - 1): e.matmul(
                    pap, l, rr, start=f, stop=s),
                deps=d, inc=(1 if i == n - 1 else 0))
        for x in readset:
            x.buf.read(tok)
        psum_r.buf.wrote(tok)
        return tok

    def barrier(self):
        for qn in ("sp", "pool", "act"):
            if self.dma_issued[qn] > self.dma_flushed[qn]:
                self._flush_dma(qn, self.dma_issued[qn])
        snap = dict(self.cnt)
        for e in QUEUES:
            for s, v in snap.items():
                if v > self.waited[e].get(s, 0):
                    self.q[e].append(("w", s, v))
                    self.waited[e][s] = v

    def replay(self, block):
        def mk(name):
            def body(eng):
                for it in self.q[name]:
                    if it[0] == "w":
                        eng.wait_ge(self.sem[it[1]], it[2])
                    elif it[0] == "d":
                        it[1](eng)
                    else:
                        it[1](eng).then_inc(self.sem[it[2]], it[3])
            return body
        block.tensor(mk("pe"))
        block.scalar(mk("act"))
        block.vector(mk("dve"))
        block.gpsimd(mk("pool"))
        block.sync(mk("sp"))


def cdma(sc, ctx, dst_r, dst_ap, f32_ap, scr_ap, key, q="pool"):
    if scr_ap is None:
        sc.dma(q, dst_ap, f32_ap, writes=[dst_r])
        return
    wb = ctx["wsbuf"].setdefault(key, Buf())
    if key in ctx["wcache"]:
        tok = sc.dma(q, dst_ap, scr_ap, writes=[dst_r], extra=wb.rdep())
        wb.read(tok)
    else:
        sc.dma(q, dst_ap, f32_ap, writes=[dst_r])
        tok = sc.dma("pool", scr_ap, f32_ap, extra=wb.wdep())
        wb.wrote(tok)
        ctx["wcache"].add(key)


class PsumPool:
    def __init__(self, nc, stack):
        self.r = [R(stack.enter_context(nc.psum_tensor(f"pb{i}", [128, 512], F32)))
                  for i in range(8)]
        self.i = 0

    def take(self, exclude=()):
        for _ in range(8):
            r = self.r[self.i]
            self.i = (self.i + 1) % 8
            if r not in exclude:
                return r
        raise RuntimeError("psum pool exhausted")


class Alloc:

    def __init__(self, nc, sc):
        self.nc = nc
        self.sc = sc
        self.n = 0

    @contextlib.contextmanager
    def scope(self):
        with contextlib.ExitStack() as st:
            def mk(name, cols, dt=F32, parts=128):
                self.n += 1
                t = st.enter_context(
                    self.nc.sbuf_tensor(f"{name}{self.n}", [parts, cols], dt))
                return R(t)
            yield mk
        self.sc.barrier()



def _lhsT(w):
    din, dout = w.shape
    assert din % 128 == 0
    return np.ascontiguousarray(
        np.asarray(w, dtype=np.float32).reshape(din // 128, 128, dout))


def _lhsT_oc(w):
    w = np.asarray(w, dtype=np.float32)
    din, dout = w.shape
    KC, OC = din // 128, dout // 128
    return np.ascontiguousarray(
        w.reshape(KC, 128, OC, 128).transpose(2, 1, 0, 3).reshape(OC, 128, din))


def _ppv(v):
    v = np.asarray(v, dtype=np.float32)
    n = v.shape[0]
    assert n % 128 == 0
    return np.ascontiguousarray(v.reshape(n // 128, 128).T)


def _pad_cols(w, cols):
    out = np.zeros((w.shape[0], cols), np.float32)
    out[:, :w.shape[1]] = w
    return out


def _pad_rows(w, rows):
    out = np.zeros((rows, w.shape[1]), np.float32)
    out[:w.shape[0]] = w
    return out


def make_consts():
    c = {}
    i = np.arange(128)[:, None]
    cc = np.arange(896)[None, :]
    c["mst_l0"] = ((cc - 384) >= i).astype(BF)
    cc2 = np.arange(1408)[None, :]
    c["mst_l1"] = (((cc2 >= i + 384) & (cc2 <= i + 896))).astype(np.uint8)
    e2 = np.zeros((128, 2), BF)
    e2[0:64, 0] = 1
    e2[64:128, 1] = 1
    c["hh_ind"] = e2
    eb = np.zeros((2, 128), BF)
    eb[0, 0:64] = 1
    eb[1, 64:128] = 1
    c["hh_bcast"] = eb
    P1 = np.zeros((64, 64), np.float32)
    for d in range(32):
        P1[d, d + 32] = -1.0
        P1[d + 32, d] = 1.0
    P2 = np.zeros((128, 128), np.float32)
    P2[0:64, 0:64] = P1
    P2[64:128, 64:128] = P1
    c["rotm"] = np.ascontiguousarray(P2.T).astype(BF)
    half = np.arange(0, 64, 2, dtype=np.float64)
    inv_freq = 1.0 / 10000.0 ** (half / 64.0)
    t = np.arange(T, dtype=np.float64)
    freqs = t[:, None] * inv_freq[None, :]
    freqs = np.concatenate([freqs, freqs], axis=1)
    base = (half + 0.4 * 64) / (1.4 * 64)
    power = (t - T // 2) / 256.0
    scale = base[None, :] ** power[:, None]
    scale = np.concatenate([scale, scale], axis=1)
    cos, sin = np.cos(freqs), np.sin(freqs)
    def tab(m):
        m = np.asarray(m, np.float32).T
        return np.ascontiguousarray(np.concatenate([m, m], axis=0))
    c["t1q"] = tab(cos * scale)
    c["t2q"] = tab(sin * scale)
    c["t1k"] = tab(cos / scale)
    c["t2k"] = tab(sin / scale)
    return c


def prep_weights(iv):
    w = {}
    w["wq"] = _lhsT(np.asarray(iv["L0_attn_Wq_w"]).T)
    w["wk"] = _lhsT(np.asarray(iv["L0_attn_Wk_w"]).T)
    w["wv"] = _lhsT(np.asarray(iv["L0_attn_Wv_w"]).T)
    w["wo"] = _lhsT(np.asarray(iv["L0_attn_Wo_w"]).T)
    w["b_l0a"] = np.ascontiguousarray(np.concatenate(
        [_ppv(iv[f"L0_attn_W{m}_b"]) for m in "qkvo"], axis=1))
    w["bv_row"] = np.asarray(iv["L0_attn_Wv_b"], np.float32).reshape(1, D)
    w["f_w1"] = _lhsT_oc(np.asarray(iv["L0_ffn_w1_w"]).T)
    w["f_w2"] = _lhsT_oc(np.asarray(iv["L0_ffn_w2_w"]).T)
    w["f_h1"] = _lhsT_oc(np.asarray(iv["L0_ffn_h1_w"]).T)
    w["f_h2"] = _lhsT_oc(np.asarray(iv["L0_ffn_h2_w"]).T)
    w["f_w3"] = _lhsT_oc(np.asarray(iv["L0_ffn_w3_w"]).T)
    w["f_s1"] = _ppv(iv["L0_ffn_w1_s"])
    w["f_s2"] = _ppv(iv["L0_ffn_w2_s"])
    w["f_sh1"] = _ppv(iv["L0_ffn_h1_s"])
    w["f_sh2"] = _ppv(iv["L0_ffn_h2_s"])
    w["f_s3"] = _ppv(iv["L0_ffn_w3_s"])
    w["a_win"] = _lhsT_oc(iv["L1_attn_W_in"])
    w["a_wx"] = _lhsT(iv["L1_attn_W_x"])
    w["a_wdt"] = _lhsT(iv["L1_attn_W_dt"])
    w["a_wout"] = _lhsT(iv["L1_attn_W_out"])
    w["a_bin"] = _ppv(iv["L1_attn_b_in"])
    w["a_bx"] = _ppv(iv["L1_attn_b_x"])
    w["a_bdt"] = _ppv(iv["L1_attn_b_dt"])
    w["a_bout"] = _ppv(iv["L1_attn_b_out"])
    w["a_wnorm"] = _ppv(iv["L1_attn_w_norm"])
    wqkv = np.asarray(iv["L1_ffn_wqkv"], np.float32)
    w["g_wqkv_qk"] = np.stack([_lhsT_oc(wqkv[l][:, 0:1024]) for l in range(2)])
    w["g_wqkv_v"] = np.stack([_lhsT(wqkv[l][:, 1024:1536]) for l in range(2)])
    wo = np.asarray(iv["L1_ffn_wo"], np.float32)
    w["g_wo"] = np.stack([_lhsT_oc(wo[l]) for l in range(2)])
    w1 = np.asarray(iv["L1_ffn_w1"], np.float32)
    HID = 2730
    w["g_w1a"] = np.stack([_lhsT_oc(_pad_cols(w1[l][:, :HID], 2816)) for l in range(2)])
    w["g_w1g"] = np.stack([_lhsT_oc(_pad_cols(w1[l][:, HID:], 2816)) for l in range(2)])
    w2 = np.asarray(iv["L1_ffn_w2"], np.float32)
    w["g_w2"] = np.stack([_lhsT_oc(_pad_rows(w2[l], 2816)) for l in range(2)])
    b1 = np.asarray(iv["L1_ffn_b1"], np.float32)
    w["g_b1a"] = np.stack([_ppv(_pad_cols(b1[l][None, :HID], 2816)[0]) for l in range(2)])
    w["g_b1g"] = np.stack([_ppv(_pad_cols(b1[l][None, HID:], 2816)[0]) for l in range(2)])
    w["g_b2"] = np.stack([_ppv(np.asarray(iv["L1_ffn_b2"])[l]) for l in range(2)])
    w["g_lng"] = np.stack([_ppv(np.asarray(iv["L1_ffn_ln_g"])[l]) for l in range(2)])
    w["g_lnb"] = np.stack([_ppv(np.asarray(iv["L1_ffn_ln_b"])[l]) for l in range(2)])
    qs = np.asarray(iv["L1_ffn_q_scale"], np.float32)
    ks = np.asarray(iv["L1_ffn_k_scale"], np.float32)
    sig = np.arange(64)
    sig = np.concatenate([sig[32:], sig[:32]])
    w["g_qs"] = np.stack([np.tile(qs[l], 2).reshape(128, 1) for l in range(2)])
    w["g_ks"] = np.stack([np.tile(ks[l], 2).reshape(128, 1) for l in range(2)])
    w["g_qss"] = np.stack([np.tile(qs[l][sig], 2).reshape(128, 1) for l in range(2)])
    w["g_kss"] = np.stack([np.tile(ks[l][sig], 2).reshape(128, 1) for l in range(2)])
    w["g_qs_row"] = np.ascontiguousarray(qs.reshape(2, 1, 64))
    w["g_ks_row"] = np.ascontiguousarray(ks.reshape(2, 1, 64))
    for k in w:
        w[k] = np.ascontiguousarray(w[k], dtype=np.float32) if w[k].dtype != np.uint8 else w[k]
    return w



def ln_stats_tile(sc, ctx, bf_fn, want_mean=True, eps_name="e5"):
    pp, ones = ctx["pp"], ctx["ones_col_bf"]
    st = ctx["lnsets"][ctx["ln_i"] % len(ctx["lnsets"])]
    ctx["ln_i"] += 1
    sq = st["sq"]
    ps, pq = pp.take(), pp.take()
    r_mu, r_tmp, r_rstd = st["mu"], st["tmp"], st["rstd"]
    mm_s = mm_q = None
    for c in range(NCH):
        ap, reads = bf_fn(c)
        slot = sq[c % 2]
        tok_sq = sc.op("act", lambda e, slot=slot, ap=ap: e.activation(
            slot.t[:], ap, AF.Square), reads=list(reads), writes=[slot])
        if want_mean:
            d = []
            for r in reads:
                d += r.buf.rdep()
            if c == 0:
                d += ps.buf.wdep()
            mm_s = sc.emit("pe", lambda e, ap=ap, f=(c == 0), s=(c == NCH - 1):
                           e.matmul(ps.t[0:1, 0:TW], ones.t[:], ap, start=f, stop=s),
                           deps=d, inc=1)
            for r in reads:
                r.buf.read(mm_s)
        d = [tok_sq] + (pq.buf.wdep() if c == 0 else [])
        mm_q = sc.emit("pe", lambda e, slot=slot, f=(c == 0), s=(c == NCH - 1):
                       e.matmul(pq.t[0:1, 0:TW], ones.t[:], slot.t[:], start=f, stop=s),
                       deps=d, inc=1)
        slot.buf.read(mm_q)
    if want_mean:
        ps.buf.wrote(mm_s)
    pq.buf.wrote(mm_q)
    if want_mean:
        sc.op("dve", lambda e: e.tensor_scalar(
            r_mu.t[:], ps.t[0:1, 0:TW], 1.0 / D, None, ALU.mult),
            reads=[ps], writes=[r_mu])
        sc.op("dve", lambda e: e.tensor_tensor(
            r_tmp.t[:], r_mu.t[:], r_mu.t[:], ALU.mult),
            reads=[r_mu], writes=[r_tmp])
        sc.op("dve", lambda e: e.scalar_tensor_tensor(
            r_tmp.t[:], pq.t[0:1, 0:TW], 1.0 / D, r_tmp.t[:], ALU.mult,
            ALU.subtract), reads=[pq], writes=[r_tmp])
    else:
        sc.op("dve", lambda e: e.tensor_scalar(
            r_tmp.t[:], pq.t[0:1, 0:TW], 1.0 / D, None, ALU.mult),
            reads=[pq], writes=[r_tmp])
    eps = ctx[eps_name]
    sc.op("act", lambda e: e.activation(
        r_rstd.t[:], r_tmp.t[:], AF.Ln, bias=eps.t[:]),
        reads=[r_tmp, eps], writes=[r_rstd])
    sc.op("act", lambda e: e.activation(
        r_rstd.t[:], r_rstd.t[:], AF.Exp, scale=-0.5), writes=[r_rstd])
    return (r_mu if want_mean else None), r_rstd


BCAST_MODE = "hilo"


def bcast_row(sc, ctx, row_r, row_ap=None, width=TW):
    pp = ctx["pp"]
    pb = pp.take()
    ap = row_ap if row_ap is not None else row_r.t[:]
    if BCAST_MODE == "fp32":
        sc.op("pe", lambda e: e.matmul(pb.t[:, 0:width], ctx["ones_row_f"].t[:],
                                       ap, start=True, stop=True),
              reads=[row_r, ctx["ones_row_f"]], writes=[pb])
        return pb
    hi, lo = ctx["bcsets"][ctx["bc_i"] % len(ctx["bcsets"])]
    ctx["bc_i"] += 1
    sc.op("dve", lambda e: e.tensor_copy(hi.t[0:1, 0:width], ap),
          reads=[row_r], writes=[hi])
    if BCAST_MODE == "hilo":
        sc.op("dve", lambda e: e.tensor_tensor(
            lo.t[0:1, 0:width], ap, hi.t[0:1, 0:width], ALU.subtract),
            reads=[row_r, hi], writes=[lo])
        items = [(pb.t[:, 0:width], ctx["ones_row_bf"].t[:], hi.t[0:1, 0:width],
                  [hi, ctx["ones_row_bf"]]),
                 (pb.t[:, 0:width], ctx["ones_row_bf"].t[:], lo.t[0:1, 0:width],
                  [lo])]
    else:
        items = [(pb.t[:, 0:width], ctx["ones_row_bf"].t[:], hi.t[0:1, 0:width],
                  [hi, ctx["ones_row_bf"]])]
    sc.mm_group(pb, items)
    return pb


def load_xtile(sc, ctx, src_ap, s, t0, dst=None):
    xt = dst if dst is not None else ctx["xt"]
    src = src_ap[s].rearrange("c p t -> p c t")[:, :, t0:t0 + TW]
    d = xt.t[:, 0:NCH * TW].rearrange("p (c t) -> p c t", c=NCH)
    sc.dma("sp", d, src, writes=[xt])
    return xt


def store_xtile(sc, ctx, dst_ap, s, t0, src_r):
    dst = dst_ap[s].rearrange("c p t -> p c t")[:, :, t0:t0 + TW]
    sap = src_r.t[:, 0:NCH * TW].rearrange("p (c t) -> p c t", c=NCH)
    sc.dma("act", dst, sap, reads=[src_r])


def cast_x_tile(sc, ctx, xt):
    cs = ctx["cast_stage"]
    for c in range(NCH):
        sc.op("dve", lambda e, c=c: e.tensor_copy(
            cs.t[:, c * TW:(c + 1) * TW], xt.t[:, c * TW:(c + 1) * TW]),
            reads=[xt], writes=[cs])
    return cs



def emit_l0a(sc, ctx, al, s, src_ap, dst_ap, wd):
    pp = ctx["pp"]
    with al.scope() as mk:
        xs = mk("xs", NCH * T, BF16)
        khg = mk("khg", 4 * T, BF16)
        vhg = mk("vhg", 16 * 520, BF16)
        qt = mk("qt", 4 * TW, BF16)
        ao = mk("ao", NCH * T, BF16)
        wst_b = [mk(f"wst{i}", 8 * 512, BF16) for i in range(2)]
        e_b = [mk(f"e{i}", TW, BF16) for i in range(4)]
        em_b = [mk(f"em{i}", TW, BF16) for i in range(4)]
        bvr = mk("bvr", D, BF16, parts=1)
        bia = mk("bia", 32)
        sc.dma("pool", bvr.t[:], wd["bv_row"], writes=[bvr])
        sc.dma("sp", bia.t[:], wd["b_l0a"], writes=[bia])

        for tt in range(NTT):
            t0 = tt * TW
            xt = load_xtile(sc, ctx, src_ap, s, t0)
            cs = cast_x_tile(sc, ctx, xt)

            def bf_fn(c, cs=cs):
                return cs.t[:, c * TW:(c + 1) * TW], [cs]
            r_mu, r_rstd = ln_stats_tile(sc, ctx, bf_fn, want_mean=True)
            pmu = bcast_row(sc, ctx, r_mu)
            pa = bcast_row(sc, ctx, r_rstd)
            for c in range(NCH):
                k = c % 2
                tf = ctx["tmp_f"][k]
                sc.op("dve", lambda e, c=c, tf=tf, xt=xt, pmu=pmu: e.tensor_tensor(
                    tf.t[:], xt.t[:, c * TW:(c + 1) * TW], pmu.t[:, 0:TW],
                    ALU.subtract), reads=[xt, pmu], writes=[tf])
                sc.op("dve", lambda e, c=c, tf=tf, t0=t0, pa=pa: e.tensor_tensor(
                    xs.t[:, c * T + t0:c * T + t0 + TW], tf.t[:], pa.t[:, 0:TW],
                    ALU.mult), reads=[tf, pa], writes=[xs])

        if s == 0:
            ctx["dump"]("xs", xs.t[:], [xs])

        for hg in range(2):
            cols = slice(hg * 512, hg * 512 + 512)
            wst = wst_b[0]
            for kc in range(NCH):
                sc.dma("pool", wst.t[:, kc * 512:(kc + 1) * 512],
                       wd["wk"][kc][:, cols], writes=[wst])
            for oc4 in range(4):
                oc = hg * 4 + oc4
                for tt in range(NTT):
                    pb = pp.take()
                    items = [(pb.t[:, 0:TW],
                              wst.t[:, kc * 512 + oc4 * 128: kc * 512 + oc4 * 128 + 128],
                              xs.t[:, kc * T + tt * TW: kc * T + (tt + 1) * TW],
                              [wst, xs]) for kc in range(NCH)]
                    sc.mm_group(pb, items)
                    sc.op("dve", lambda e, pb=pb, oc4=oc4, tt=tt, oc=oc:
                          e.tensor_scalar(
                              khg.t[:, oc4 * T + tt * TW: oc4 * T + (tt + 1) * TW],
                              pb.t[:, 0:TW], 1.0, bia.t[:, 8 + oc:9 + oc],
                              ALU.mult, ALU.add),
                          reads=[pb, bia], writes=[khg])
            wst = wst_b[1]
            for kc in range(NCH):
                sc.dma("pool", wst.t[:, kc * 512:(kc + 1) * 512],
                       wd["wv"][kc][:, cols], writes=[wst])
            onesap = vhg.t[:, 0:16 * 520].rearrange(
                "p (tc h o) -> p tc h o", tc=16, h=8)[:, :, :, 64:65]
            sc.op("dve", lambda e: e.memset(onesap, 1.0), writes=[vhg])
            for tc in range(16):
                pb = pp.take()
                items = [(pb.t[:, 0:512],
                          xs.t[:, kc * T + tc * 128: kc * T + tc * 128 + 128],
                          wst.t[:, kc * 512:(kc + 1) * 512],
                          [wst, xs]) for kc in range(NCH)]
                items.append((pb.t[:, 0:512], ctx["ones_row_bf"].t[:],
                              bvr.t[0:1, hg * 512: hg * 512 + 512], [bvr]))
                sc.mm_group(pb, items)
                d = vhg.t[:, tc * 520: tc * 520 + 520].rearrange(
                    "p (h o) -> p h o", h=8)[:, :, 0:64]
                srcap = pb.t[:, 0:512].rearrange("p (h o) -> p h o", h=8)
                sc.op("dve", lambda e, d=d, srcap=srcap: e.tensor_copy(d, srcap),
                      reads=[pb], writes=[vhg])
            if s == 0 and hg == 0:
                ctx["dump"]("khg0", khg.t[:], [khg])
                ctx["dump"]("vhg0", vhg.t[:], [vhg])
            wst = wst_b[0]
            for kc in range(NCH):
                sc.dma("pool", wst.t[:, kc * 512:(kc + 1) * 512],
                       wd["wq"][kc][:, cols], writes=[wst])
            for qi in range(NTT):
                for oc4 in range(4):
                    oc = hg * 4 + oc4
                    pb = pp.take()
                    items = [(pb.t[:, 0:TW],
                              wst.t[:, kc * 512 + oc4 * 128: kc * 512 + oc4 * 128 + 128],
                              xs.t[:, kc * T + qi * TW: kc * T + (qi + 1) * TW],
                              [wst, xs]) for kc in range(NCH)]
                    sc.mm_group(pb, items)
                    sc.op("dve", lambda e, pb=pb, oc4=oc4, oc=oc:
                          e.tensor_scalar(
                              qt.t[:, oc4 * TW:(oc4 + 1) * TW], pb.t[:, 0:TW],
                              1.0, bia.t[:, oc:oc + 1], ALU.mult, ALU.add),
                          reads=[pb, bia], writes=[qt])
                if s == 0 and hg == 0 and qi == 0:
                    ctx["dump"]("qt00", qt.t[:], [qt])
                for hh in range(8):
                    h = hg * 8 + hh
                    rbase = (hh % 2) * 64
                    kcc = hh // 2
                    pav = pp.take()
                    nkj = 4 * qi + 4
                    for b0 in range(0, nkj, 4):
                        batch = list(range(b0, min(b0 + 4, nkj)))
                        stiles = []
                        for kj in batch:
                            pbs = pp.take(exclude=[pav])
                            kap = khg.t[rbase:rbase + 64,
                                        kcc * T + kj * 128: kcc * T + kj * 128 + 128]
                            qap = qt.t[rbase:rbase + 64, kcc * TW: (kcc + 1) * TW]
                            sc.mm_group(pbs, [(pbs.t[:, 0:TW], kap, qap, [khg, qt])])
                            stiles.append((kj, pbs))
                        esrcs = []
                        for kj, pbs in stiles:
                            eb = e_b[kj % 4]
                            sc.op("act", lambda e, pbs=pbs, eb=eb: e.activation(
                                eb.t[:], pbs.t[:, 0:TW], AF.Exp, scale=1.0 / 32.0),
                                reads=[pbs], writes=[eb])
                            if kj >= 4 * qi:
                                r = kj - 4 * qi
                                off = 384 - 128 * r
                                em = em_b[kj % 4]
                                sc.op("dve", lambda e, eb=eb, off=off, em=em:
                                      e.tensor_tensor(em.t[:], eb.t[:],
                                                      ctx["mst0"].t[:, off:off + TW],
                                                      ALU.mult),
                                      reads=[eb, ctx["mst0"]], writes=[em])
                                esrcs.append((kj, em))
                            else:
                                esrcs.append((kj, eb))
                        for kj, esrc in esrcs:
                            vap = vhg.t[:, kj * 520 + hh * 65: kj * 520 + hh * 65 + 65]
                            d = [] if kj else list(pav.buf.wdep())
                            d += vhg.buf.rdep() + esrc.buf.rdep()
                            tok = sc.emit("pe", lambda e, vap=vap, esrc=esrc,
                                          pav=pav, f=(kj == 0),
                                          st=(kj == nkj - 1): e.matmul(
                                              pav.t[0:65, 0:TW], vap, esrc.t[:],
                                              start=f, stop=st),
                                          deps=d, inc=1)
                            vhg.buf.read(tok)
                            esrc.buf.read(tok)
                            if kj == nkj - 1:
                                pav.buf.wrote(tok)
                    r_rec = ctx["r_recs"][hh % 3]
                    sc.op("act", lambda e, pav=pav, r_rec=r_rec: e.activation(
                        r_rec.t[:], pav.t[64:65, 0:TW], AF.Ln),
                        reads=[pav], writes=[r_rec])
                    sc.op("act", lambda e, r_rec=r_rec: e.activation(
                        r_rec.t[:], r_rec.t[:], AF.Exp, scale=-1.0),
                        writes=[r_rec])
                    pbc = bcast_row(sc, ctx, r_rec)
                    tf = ctx["tmp_f"][hh % 4]
                    sc.op("dve", lambda e, pav=pav, tf=tf: e.tensor_copy(
                        tf.t[0:64, :], pav.t[0:64, 0:TW]),
                        reads=[pav], writes=[tf])
                    aod = ao.t[rbase:rbase + 64,
                               (hg * 4 + kcc) * T + qi * TW: (hg * 4 + kcc) * T + (qi + 1) * TW]
                    sc.op("dve", lambda e, tf=tf, aod=aod, pbc=pbc: e.tensor_tensor(
                        aod, tf.t[0:64, :], pbc.t[0:64, 0:TW], ALU.mult),
                        reads=[tf, pbc], writes=[ao])

        if s == 0:
            ctx["dump"]("ao", ao.t[:], [ao])
        for kc in range(NCH):
            b = wst_b[kc // 4]
            sc.dma("pool", b.t[:, (kc % 4) * 1024:(kc % 4 + 1) * 1024],
                   wd["wo"][kc], writes=[b])
        for tt in range(NTT):
            t0 = tt * TW
            xt = load_xtile(sc, ctx, src_ap, s, t0)
            for oc in range(NCH):
                pb = pp.take()
                items = [(pb.t[:, 0:TW],
                          wst_b[kc // 4].t[:, (kc % 4) * 1024 + oc * 128:
                                           (kc % 4) * 1024 + oc * 128 + 128],
                          ao.t[:, kc * T + t0: kc * T + t0 + TW],
                          [wst_b[kc // 4], ao]) for kc in range(NCH)]
                sc.mm_group(pb, items)
                sc.op("dve", lambda e, pb=pb, oc=oc: e.scalar_tensor_tensor(
                    xt.t[:, oc * TW:(oc + 1) * TW], pb.t[:, 0:TW],
                    bia.t[:, 24 + oc:25 + oc], xt.t[:, oc * TW:(oc + 1) * TW],
                    ALU.add, ALU.add), reads=[pb, bia], writes=[xt])
            store_xtile(sc, ctx, dst_ap, s, t0, xt)



WEIGHT_SHAPES = {
    "wq": [8, 128, 1024], "wk": [8, 128, 1024], "wv": [8, 128, 1024],
    "wo": [8, 128, 1024], "b_l0a": [128, 32], "bv_row": [1, 1024],
    "f_w1": [32, 128, 1024], "f_w2": [32, 128, 1024],
    "f_h1": [32, 128, 4096], "f_h2": [32, 128, 4096], "f_w3": [8, 128, 4096],
    "f_s1": [128, 32], "f_s2": [128, 32], "f_sh1": [128, 32],
    "f_sh2": [128, 32], "f_s3": [128, 8],
    "a_win": [6, 128, 1024], "a_wx": [2, 128, 256], "a_wdt": [2, 128, 256],
    "a_wout": [2, 128, 1024], "a_bin": [128, 6], "a_bx": [128, 2],
    "a_bdt": [128, 2], "a_bout": [128, 8], "a_wnorm": [128, 8],
    "g_wqkv_qk": [2, 8, 128, 1024], "g_wqkv_v": [2, 8, 128, 512],
    "g_wo": [2, 8, 128, 512],
    "g_w1a": [2, 22, 128, 1024], "g_w1g": [2, 22, 128, 1024],
    "g_w2": [2, 8, 128, 2816],
    "g_b1a": [2, 128, 22], "g_b1g": [2, 128, 22], "g_b2": [2, 128, 8],
    "g_lng": [2, 128, 8], "g_lnb": [2, 128, 8],
    "g_qs": [2, 128, 1], "g_ks": [2, 128, 1],
    "g_qss": [2, 128, 1], "g_kss": [2, 128, 1],
    "g_qs_row": [2, 1, 64], "g_ks_row": [2, 1, 64],
}
CONST_SHAPES = {
    "mst_l0": ([128, 896], BF16), "mst_l1": ([128, 1408], U8),
    "hh_ind": ([128, 2], BF16), "hh_bcast": ([2, 128], BF16),
    "rotm": ([128, 128], BF16),
    "t1q": ([128, T], F32), "t2q": ([128, T], F32),
    "t1k": ([128, T], F32), "t2k": ([128, T], F32),
}
BLOCK_WEIGHTS = {
    "l0a": ["wq", "wk", "wv", "wo", "b_l0a", "bv_row"],
    "l0f": ["f_w1", "f_w2", "f_h1", "f_h2", "f_w3",
            "f_s1", "f_s2", "f_sh1", "f_sh2", "f_s3"],
    "l1a": ["a_win", "a_wx", "a_wdt", "a_wout", "a_bin", "a_bx", "a_bdt",
            "a_bout", "a_wnorm"],
    "l1f": ["g_wqkv_qk", "g_wqkv_v", "g_wo", "g_w1a", "g_w1g", "g_w2", "g_b1a", "g_b1g",
            "g_b2", "g_lng", "g_lnb", "g_qs", "g_ks", "g_qss", "g_kss",
            "g_qs_row", "g_ks_row"],
}
BLOCK_CONSTS = {
    "l0a": ["mst_l0"],
    "l0f": [],
    "l1a": [],
    "l1f": ["mst_l1", "hh_ind", "hh_bcast", "rotm", "t1q", "t2q", "t1k", "t2k"],
}

EMITTERS = {}


def build_program(blocks, dumps=()):
    nc = bass.Bass("TRN2", target_bir_lowering=False)
    xin = nc.dram_tensor("xin", [SEQS, NCH, 128, T], F32, kind="ExternalInput").ap()
    xout = nc.dram_tensor("xout", [SEQS, NCH, 128, T], F32, kind="ExternalOutput").ap()
    dump_aps = {n: nc.dram_tensor("dbg_" + n, shape, dt, kind="ExternalOutput").ap()
                for n, shape, dt in dumps}
    need_mid = len(blocks) > 1
    xh = nc.dram_tensor("xh", [SEQS, NCH, 128, T], F32).ap() if need_mid else None
    zh = (nc.dram_tensor("zh", [NCH, 128, T], F32).ap()
          if ("l1f" in blocks or "l1a" in blocks) else None)
    names = []
    wd = {}
    for b in blocks:
        names += BLOCK_WEIGHTS[b] + BLOCK_CONSTS[b]
    names = sorted(set(names))
    for n in names:
        if n in WEIGHT_SHAPES:
            wd[n] = nc.dram_tensor(n, WEIGHT_SHAPES[n], F32, kind="ExternalInput").ap()
        else:
            shape, dt = CONST_SHAPES[n]
            wd[n] = nc.dram_tensor(n, shape, dt, kind="ExternalInput").ap()
    scr = {}
    for n in ("f_w1", "f_w2", "f_h1", "f_h2", "f_w3", "a_win",
              "g_w1a", "g_w1g", "g_w2"):
        if n in names:
            scr[n] = nc.dram_tensor("scr_" + n, WEIGHT_SHAPES[n], BF16).ap()

    with contextlib.ExitStack() as stack:
        sc = Sched(nc, stack)
        pp = PsumPool(nc, stack)
        al = Alloc(nc, sc)
        mkp = lambda name, shape, dt=F32: R(stack.enter_context(
            nc.sbuf_tensor(name, shape, dt)))
        ctx = {"pp": pp, "sc": sc, "nc": nc,
               "wcache": set(), "wsbuf": {}, "wscr": scr}

        def dump(name, ap, reads=()):
            if name in dump_aps:
                sc.dma("sp", dump_aps[name], ap, reads=list(reads))
        ctx["dump"] = dump
        ctx["ones_col_bf"] = mkp("ones_col_bf", [128, 1], BF16)
        ctx["ones_row_bf"] = mkp("ones_row_bf", [1, 128], BF16)
        ctx["zeros_bf"] = mkp("zeros_bf", [128, TW], BF16)
        ctx["e5"] = mkp("e5", [1, 1], F32)
        ctx["e8"] = mkp("e8", [1, 1], F32)
        for nm, val in (("ones_col_bf", 1.0),
                        ("ones_row_bf", 1.0), ("zeros_bf", 0.0),
                        ("e5", 1e-5), ("e8", 1e-8)):
            r = ctx[nm]
            sc.op("pool", lambda e, r=r, val=val: e.memset(r.t[:], val), writes=[r])
        ctx["r_rec"] = mkp("r_rec", [1, TW], F32)
        _sq = [mkp(f"lnsq{j}", [128, TW], BF16) for j in range(2)]
        ctx["lnsets"] = [
            {"mu": mkp(f"lnmu{i}", [1, TW], F32),
             "tmp": mkp(f"lntmp{i}", [1, TW], F32),
             "rstd": mkp(f"lnrs{i}", [1, TW], F32),
             "sq": _sq}
            for i in range(2)]
        ctx["ln_i"] = 0
        ctx["bcsets"] = [(mkp(f"bch{i}", [1, TW], BF16),
                          mkp(f"bcl{i}", [1, TW], BF16)) for i in range(2)]
        ctx["bc_i"] = 0
        ctx["tmp_f"] = [mkp(f"tmpf{i}", [128, TW], F32) for i in range(4)]
        ctx["r_recs"] = [ctx["r_rec"]] + [mkp(f"r_rec{i}", [1, TW], F32)
                                          for i in range(1, 3)]
        ctx["xt"] = mkp("xt", [128, NCH * TW], F32)
        ctx["zh_buf"] = Buf()
        ctx["cast_stage"] = mkp("cast_stage", [128, NCH * TW], BF16)
        if zh is not None:
            wd["zh"] = zh
        for n in names:
            if n in CONST_SHAPES and n not in ("t1q", "t2q", "t1k", "t2k"):
                shape, dt = CONST_SHAPES[n]
                key = {"mst_l0": "mst0", "mst_l1": "mst1"}.get(n, n)
                ctx[key] = mkp("c_" + n, shape, dt)
                sc.dma("sp", ctx[key].t[:], wd[n], writes=[ctx[key]])
        sc.barrier()

        for s in range(SEQS):
            for i, b in enumerate(blocks):
                src = xin if i == 0 else xh
                dst = xout if i == len(blocks) - 1 else xh
                EMITTERS[b](sc, ctx, al, s, src, dst, wd)
                sc.barrier()
        sc.barrier()
        with nc.Block() as block:
            sc.replay(block)
    return nc, names


def shard_x(x_full):
    x = np.asarray(x_full, dtype=np.float32)
    shards = []
    for c in range(NCORES):
        xs = x[c * SEQS:(c + 1) * SEQS]
        xs = np.ascontiguousarray(xs.transpose(0, 2, 1))
        shards.append(xs.reshape(SEQS, NCH, 128, T))
    return shards


def unshard_x(outs):
    xs = []
    for o in outs:
        o = np.asarray(o, dtype=np.float32).reshape(SEQS, D, T)
        xs.append(o.transpose(0, 2, 1))
    return np.ascontiguousarray(np.concatenate(xs, axis=0))


def run_blocks(block_groups, inputs):
    consts = make_consts()
    wts = prep_weights(inputs)
    host = {**consts, **wts}
    cur = shard_x(inputs["L0_attn_x"])
    for blocks in block_groups:
        nc, names = build_program(blocks)
        in_maps = []
        for c in range(NCORES):
            m = {"xin": cur[c]}
            for n in names:
                m[n] = host[n]
            in_maps.append(m)
        res = run_bass_kernel_spmd(nc, in_maps, core_ids=list(range(NCORES)))
        cur = [np.asarray(res.results[c]["xout"]) for c in range(NCORES)]
    return unshard_x(cur)


def kernel(**inputs):
    out = run_blocks([("l0a", "l0f", "l1a", "l1f")], inputs)
    return out


EMITTERS["l0a"] = emit_l0a



def mm_group2(sc, p0, a0, p1, a1, items):
    n = len(items)
    readset = []
    tok = None
    for i, (l, r0, r1, reads) in enumerate(items):
        d = []
        if i == 0:
            d += p0.buf.wdep() + p1.buf.wdep()
        for x in reads:
            d += x.buf.rdep()
            readset.append(x)
        sc.emit("pe", lambda e, l=l, r0=r0, a0=a0, f=(i == 0), st=(i == n - 1):
                e.matmul(a0, l, r0, start=f, stop=st), deps=d, inc=0)
        tok = sc.emit("pe", lambda e, l=l, r1=r1, a1=a1, f=(i == 0),
                      st=(i == n - 1): e.matmul(a1, l, r1, start=f, stop=st),
                      deps=[], inc=1)
    for x in readset:
        x.buf.read(tok)
    p0.buf.wrote(tok)
    p1.buf.wrote(tok)
    return tok


def emit_l0f(sc, ctx, al, s, src_ap, dst_ap, wd):
    pp = ctx["pp"]
    TT2 = 2 * TW
    with al.scope() as mk:
        scl = mk("scl", 136)
        sc.dma("sp", scl.t[:, 0:32], wd["f_s1"], writes=[scl])
        sc.dma("sp", scl.t[:, 32:64], wd["f_s2"], writes=[scl])
        sc.dma("sp", scl.t[:, 64:96], wd["f_sh1"], writes=[scl])
        sc.dma("sp", scl.t[:, 96:128], wd["f_sh2"], writes=[scl])
        sc.dma("sp", scl.t[:, 128:136], wd["f_s3"], writes=[scl])
        g1 = mk("g1", 32 * TT2, BF16)

        for t2 in range(T // TT2):
            base = t2 * TT2
            scW = al.scope()
            mkW = scW.__enter__()
            xs2 = mkW("xs2", NCH * TT2, BF16)
            w12 = [mkW(f"w12_{i}", 8 * 128, BF16) for i in range(4)]
            for sub in range(2):
                t0 = base + sub * TW
                xt = load_xtile(sc, ctx, src_ap, s, t0)
                cs = cast_x_tile(sc, ctx, xt)

                def bf_fn(c, cs=cs):
                    return cs.t[:, c * TW:(c + 1) * TW], [cs]
                r_mu, r_rstd = ln_stats_tile(sc, ctx, bf_fn, want_mean=True)
                pmu = bcast_row(sc, ctx, r_mu)
                pa = bcast_row(sc, ctx, r_rstd)
                for c in range(NCH):
                    tf = ctx["tmp_f"][c % 2]
                    sc.op("dve", lambda e, c=c, tf=tf, xt=xt, pmu=pmu:
                          e.tensor_tensor(tf.t[:], xt.t[:, c * TW:(c + 1) * TW],
                                          pmu.t[:, 0:TW], ALU.subtract),
                          reads=[xt, pmu], writes=[tf])
                    sc.op("dve", lambda e, c=c, tf=tf, pa=pa, sub=sub, xs2=xs2:
                          e.tensor_tensor(
                              xs2.t[:, c * TT2 + sub * TW: c * TT2 + sub * TW + TW],
                              tf.t[:], pa.t[:, 0:TW], ALU.mult),
                          reads=[tf, pa], writes=[xs2])
            for oc in range(32):
                sa = w12[oc % 2]
                sb = w12[2 + oc % 2]
                cdma(sc, ctx, sa, sa.t[:], wd["f_w1"][oc],
                     ctx["wscr"]["f_w1"][oc], ("f_w1", oc))
                cdma(sc, ctx, sb, sb.t[:], wd["f_w2"][oc],
                     ctx["wscr"]["f_w2"][oc], ("f_w2", oc))
                pa0 = pp.take()
                pa1 = pp.take(exclude=[pa0])
                items = [(sa.t[:, kc * 128:(kc + 1) * 128],
                          xs2.t[:, kc * TT2: kc * TT2 + TW],
                          xs2.t[:, kc * TT2 + TW: kc * TT2 + TT2],
                          [sa, xs2]) for kc in range(NCH)]
                mm_group2(sc, pa0, pa0.t[:, 0:TW], pa1, pa1.t[:, 0:TW], items)
                pg0 = pp.take(exclude=[pa0, pa1])
                pg1 = pp.take(exclude=[pa0, pa1, pg0])
                items = [(sb.t[:, kc * 128:(kc + 1) * 128],
                          xs2.t[:, kc * TT2: kc * TT2 + TW],
                          xs2.t[:, kc * TT2 + TW: kc * TT2 + TT2],
                          [sb, xs2]) for kc in range(NCH)]
                mm_group2(sc, pg0, pg0.t[:, 0:TW], pg1, pg1.t[:, 0:TW], items)
                for sub, pa_, pg_ in ((0, pa0, pg0), (1, pa1, pg1)):
                    tf = ctx["tmp_f"][2 * (oc % 2) + sub]
                    sc.op("act", lambda e, pa_=pa_, tf=tf, oc=oc: e.activation(
                        tf.t[:], pa_.t[:, 0:TW], AF.Silu,
                        scale=scl.t[:, oc:oc + 1]), reads=[pa_, scl], writes=[tf])
                    sc.op("dve", lambda e, pg_=pg_, tf=tf, oc=oc, sub=sub:
                          e.scalar_tensor_tensor(
                              g1.t[:, oc * TT2 + sub * TW: oc * TT2 + sub * TW + TW],
                              pg_.t[:, 0:TW], scl.t[:, 32 + oc:33 + oc], tf.t[:],
                              ALU.mult, ALU.mult),
                          reads=[pg_, tf, scl], writes=[g1])
            scW.__exit__(None, None, None)

            scH = al.scope()
            mkH = scH.__enter__()
            g2 = mkH("g2", 32 * TT2, BF16)
            hstg = [mkH(f"hstg{i}", 32 * 128, BF16) for i in range(2)]
            for wname, scol, src_g, dst_g in (("f_h1", 64, g1, g2),
                                              ("f_h2", 96, g2, g1)):
                for oc in range(32):
                    st_ = hstg[oc % 2]
                    cdma(sc, ctx, st_, st_.t[:], wd[wname][oc],
                         ctx["wscr"][wname][oc], (wname, oc))
                    p0 = pp.take()
                    p1 = pp.take(exclude=[p0])
                    items = [(st_.t[:, kc * 128:(kc + 1) * 128],
                              src_g.t[:, kc * TT2: kc * TT2 + TW],
                              src_g.t[:, kc * TT2 + TW: kc * TT2 + TT2],
                              [st_, src_g]) for kc in range(32)]
                    mm_group2(sc, p0, p0.t[:, 0:TW], p1, p1.t[:, 0:TW], items)
                    for sub, p_ in ((0, p0), (1, p1)):
                        sc.op("act", lambda e, p_=p_, oc=oc, scol=scol, sub=sub,
                              dst_g=dst_g: e.activation(
                                  dst_g.t[:, oc * TT2 + sub * TW:
                                          oc * TT2 + sub * TW + TW],
                                  p_.t[:, 0:TW], AF.Gelu,
                                  scale=scl.t[:, scol + oc:scol + oc + 1]),
                              reads=[p_, scl], writes=[dst_g])
            for sub in range(2):
                t0 = base + sub * TW
                xt = load_xtile(sc, ctx, src_ap, s, t0)
                for oc in range(NCH):
                    st_ = hstg[oc % 2]
                    cdma(sc, ctx, st_, st_.t[:], wd["f_w3"][oc],
                         ctx["wscr"]["f_w3"][oc], ("f_w3", oc))
                    pb_ = pp.take()
                    items = [(pb_.t[:, 0:TW], st_.t[:, kc * 128:(kc + 1) * 128],
                              g1.t[:, kc * TT2 + sub * TW: kc * TT2 + sub * TW + TW],
                              [st_, g1]) for kc in range(32)]
                    sc.mm_group(pb_, items)
                    sc.op("dve", lambda e, pb_=pb_, oc=oc, xt=xt:
                          e.scalar_tensor_tensor(
                              xt.t[:, oc * TW:(oc + 1) * TW], pb_.t[:, 0:TW],
                              scl.t[:, 128 + oc:129 + oc],
                              xt.t[:, oc * TW:(oc + 1) * TW],
                              ALU.mult, ALU.add), reads=[pb_, scl], writes=[xt])
                store_xtile(sc, ctx, dst_ap, s, t0, xt)
            scH.__exit__(None, None, None)


EMITTERS["l0f"] = emit_l0f



def emit_l1a(sc, ctx, al, s, src_ap, dst_ap, wd):
    pp = ctx["pp"]
    with al.scope() as mk:
        ub = mk("ub", 2 * T, BF16)
        zb = mk("zb", 2 * T, BF16)
        sg = mk("sg", 2 * T)
        ut = mk("ut", 2 * T)
        dt_ = mk("dt_", 2 * T)
        hs = mk("hs", 2 * T)
        xacc_b = [mk(f"xacc{i}", NCH * TW) for i in range(2)]
        xst_b = [mk(f"xst{i}", NCH * TW, BF16) for i in range(2)]
        xnt_b = [ctx["cast_stage"], mk("xnt1", NCH * TW, BF16)]
        win = [mk(f"win{i}", 8 * 128, BF16) for i in range(2)]
        wsm = mk("wsm", 512 + 512 + 2048, BF16)
        bi = mk("bi", 26)
        for kc in range(2):
            sc.dma("pool", wsm.t[:, kc * 256:(kc + 1) * 256], wd["a_wx"][kc],
                   writes=[wsm])
            sc.dma("pool", wsm.t[:, 512 + kc * 256: 512 + (kc + 1) * 256],
                   wd["a_wdt"][kc], writes=[wsm])
            sc.dma("pool", wsm.t[:, 1024 + kc * 1024: 1024 + (kc + 1) * 1024],
                   wd["a_wout"][kc], writes=[wsm])
        sc.dma("sp", bi.t[:, 0:6], wd["a_bin"], writes=[bi])
        sc.dma("sp", bi.t[:, 6:8], wd["a_bx"], writes=[bi])
        sc.dma("sp", bi.t[:, 8:10], wd["a_bdt"], writes=[bi])
        sc.dma("sp", bi.t[:, 10:18], wd["a_bout"], writes=[bi])
        sc.dma("sp", bi.t[:, 18:26], wd["a_wnorm"], writes=[bi])
        ysb = ub

        for tt in range(NTT):
            t0 = tt * TW
            xacc = xacc_b[tt % 2]
            xst = xst_b[tt % 2]
            xnt = xnt_b[tt % 2]
            srcp = src_ap[s].rearrange("c p t -> p c t")[:, :, t0:t0 + TW]
            d = xacc.t[:, 0:NCH * TW].rearrange("p (c t) -> p c t", c=NCH)
            sc.dma("sp", d[:, 0:4], srcp[:, 0:4], writes=[xacc])
            sc.dma("pool", d[:, 4:8], srcp[:, 4:8], writes=[xacc])
            for c in range(NCH):
                sc.op("dve", lambda e, c=c, xacc=xacc, xst=xst: e.tensor_copy(
                    xst.t[:, c * TW:(c + 1) * TW],
                    xacc.t[:, c * TW:(c + 1) * TW]),
                    reads=[xacc], writes=[xst])

            def bf_fn(c, xst=xst):
                return xst.t[:, c * TW:(c + 1) * TW], [xst]
            r_mu, r_rstd = ln_stats_tile(sc, ctx, bf_fn, want_mean=True)
            pmu = bcast_row(sc, ctx, r_mu)
            pa = bcast_row(sc, ctx, r_rstd)
            for c in range(NCH):
                tf = ctx["tmp_f"][c % 2]
                tf2 = ctx["tmp_f"][2 + c % 2]
                sc.op("dve", lambda e, c=c, tf=tf, xacc=xacc, pmu=pmu:
                      e.tensor_tensor(tf.t[:], xacc.t[:, c * TW:(c + 1) * TW],
                                      pmu.t[:, 0:TW], ALU.subtract),
                      reads=[xacc, pmu], writes=[tf])
                sc.op("dve", lambda e, tf=tf, tf2=tf2, pa=pa: e.tensor_tensor(
                    tf2.t[:], tf.t[:], pa.t[:, 0:TW], ALU.mult),
                    reads=[tf, pa], writes=[tf2])
                sc.op("dve", lambda e, c=c, tf2=tf2, xst=xst: e.tensor_copy(
                    xst.t[:, c * TW:(c + 1) * TW], tf2.t[:]),
                    reads=[tf2], writes=[xst])
                sc.op("dve", lambda e, c=c, tf2=tf2, xacc=xacc: e.tensor_tensor(
                    xacc.t[:, c * TW:(c + 1) * TW],
                    xacc.t[:, c * TW:(c + 1) * TW], tf2.t[:], ALU.add),
                    reads=[tf2], writes=[xacc])

            def bf2(c, xst=xst):
                return xst.t[:, c * TW:(c + 1) * TW], [xst]
            _, r_rstd2 = ln_stats_tile(sc, ctx, bf2, want_mean=False,
                                       eps_name="e8")
            pa2 = bcast_row(sc, ctx, r_rstd2)
            for c in range(NCH):
                sc.op("dve", lambda e, c=c, pa2=pa2, xst=xst, xnt=xnt:
                      e.scalar_tensor_tensor(
                          xnt.t[:, c * TW:(c + 1) * TW],
                          xst.t[:, c * TW:(c + 1) * TW],
                          bi.t[:, 18 + c:19 + c], pa2.t[:, 0:TW],
                          ALU.mult, ALU.mult),
                      reads=[xst, bi, pa2], writes=[xnt])
            for oc in range(6):
                wst_ = win[oc % 2]
                cdma(sc, ctx, wst_, wst_.t[:], wd["a_win"][oc],
                     ctx["wscr"]["a_win"][oc], ("a_win", oc))
                pb = pp.take()
                items = [(pb.t[:, 0:TW],
                          wst_.t[:, kc * 128:(kc + 1) * 128],
                          xnt.t[:, kc * TW:(kc + 1) * TW], [wst_, xnt])
                         for kc in range(NCH)]
                sc.mm_group(pb, items)
                if oc < 2:
                    dst, fn = ub, AF.Identity
                elif oc < 4:
                    dst, fn = zb, AF.Identity
                else:
                    dst, fn = sg, AF.Sigmoid
                co = oc % 2
                sc.op("act", lambda e, pb=pb, dst=dst, fn=fn, co=co, t0=t0, oc=oc:
                      e.activation(dst.t[:, co * T + t0:co * T + t0 + TW],
                                   pb.t[:, 0:TW], fn, bias=bi.t[:, oc:oc + 1],
                                   scale=1.0),
                      reads=[pb, bi], writes=[dst])
            for (srcb, wof, bof, fn, dst) in ((ub, 0, 6, AF.Tanh, ut),
                                              (zb, 512, 8, AF.Sigmoid, dt_)):
                for oc2 in range(2):
                    pb = pp.take()
                    items = [(pb.t[:, 0:TW],
                              wsm.t[:, wof + kc2 * 256 + oc2 * 128:
                                    wof + kc2 * 256 + oc2 * 128 + 128],
                              srcb.t[:, kc2 * T + t0: kc2 * T + t0 + TW],
                              [wsm, srcb]) for kc2 in range(2)]
                    sc.mm_group(pb, items)
                    sc.op("act", lambda e, pb=pb, fn=fn, dst=dst, oc2=oc2, t0=t0,
                          bof=bof: e.activation(
                              dst.t[:, oc2 * T + t0: oc2 * T + t0 + TW],
                              pb.t[:, 0:TW], fn,
                              bias=bi.t[:, bof + oc2:bof + oc2 + 1],
                              scale=1.0), reads=[pb, bi], writes=[dst])
            for c in range(2):
                lo = c * T + t0
                sc.op("dve", lambda e, lo=lo: e.tensor_tensor(
                    ut.t[:, lo:lo + TW], dt_.t[:, lo:lo + TW],
                    ut.t[:, lo:lo + TW], ALU.mult), reads=[dt_], writes=[ut])
                sc.op("dve", lambda e, lo=lo: e.tensor_scalar(
                    dt_.t[:, lo:lo + TW], dt_.t[:, lo:lo + TW],
                    -1.0, 1.0, ALU.mult, ALU.add), writes=[dt_])
                init = 0.0 if tt == 0 else hs.t[:, lo - 1:lo]
                sc.op("dve", lambda e, lo=lo, init=init: e.tensor_tensor_scan(
                    hs.t[:, lo:lo + TW], dt_.t[:, lo:lo + TW],
                    ut.t[:, lo:lo + TW], init, ALU.mult, ALU.add),
                    reads=[dt_, ut], writes=[hs])
                sc.op("dve", lambda e, lo=lo: e.tensor_tensor(
                    ysb.t[:, lo:lo + TW], hs.t[:, lo:lo + TW],
                    sg.t[:, lo:lo + TW], ALU.mult), reads=[hs, sg], writes=[ysb])
            for oc in range(NCH):
                pb = pp.take()
                items = [(pb.t[:, 0:TW],
                          wsm.t[:, 1024 + kc2 * 1024 + oc * 128:
                                1024 + kc2 * 1024 + oc * 128 + 128],
                          ysb.t[:, kc2 * T + t0: kc2 * T + t0 + TW],
                          [wsm, ysb]) for kc2 in range(2)]
                sc.mm_group(pb, items)
                sc.op("dve", lambda e, pb=pb, oc=oc, xacc=xacc:
                      e.scalar_tensor_tensor(
                          xacc.t[:, oc * TW:(oc + 1) * TW], pb.t[:, 0:TW],
                          bi.t[:, 10 + oc:11 + oc],
                          xacc.t[:, oc * TW:(oc + 1) * TW],
                          ALU.add, ALU.add), reads=[pb, bi], writes=[xacc])
            store_xtile(sc, ctx, dst_ap, s, t0, xacc)


EMITTERS["l1a"] = emit_l1a



def store_ztile(sc, ctx, zh_ap, zh_buf, t0, src_r):
    dst = zh_ap.rearrange("c p t -> p c t")[:, :, t0:t0 + TW]
    sap = src_r.t[:, 0:NCH * TW].rearrange("p (c t) -> p c t", c=NCH)
    sc.dma("act", dst, sap, reads=[src_r], extra=zh_buf.wdep())
    tok = src_r.buf.rs[-1]
    zh_buf.wrote(tok)


def load_ztile(sc, ctx, zh_ap, zh_buf, t0, dst):
    srcp = zh_ap.rearrange("c p t -> p c t")[:, :, t0:t0 + TW]
    d = dst.t[:, 0:NCH * TW].rearrange("p (c t) -> p c t", c=NCH)
    sc.dma("sp", d, srcp, writes=[dst], extra=zh_buf.rdep())
    zh_buf.read(dst.buf.w)


def load_zhalf(sc, ctx, zh_ap, zh_buf, base, width, dst):
    srcp = zh_ap.rearrange("c p t -> p c t")[:, :, base:base + width]
    d = dst.t[:, 0:NCH * width].rearrange("p (c t) -> p c t", c=NCH)
    sc.dma("sp", d, srcp, writes=[dst], extra=zh_buf.rdep())
    zh_buf.read(dst.buf.w)


def store_zhalf(sc, ctx, zh_ap, zh_buf, base, width, src_r):
    dst = zh_ap.rearrange("c p t -> p c t")[:, :, base:base + width]
    sap = src_r.t[:, 0:NCH * width].rearrange("p (c t) -> p c t", c=NCH)
    sc.dma("act", dst, sap, reads=[src_r], extra=zh_buf.wdep())
    zh_buf.wrote(src_r.buf.rs[-1])


def emit_shift_cast_half(sc, ctx, zh_ap, zh_buf, base, width, dst):
    for c in range(NCH):
        col = c * width
        if c < 4:
            tok = sc.dma("pool", dst.t[:, col:col + width],
                         zh_ap[c][:, base:base + width], writes=[dst],
                         extra=zh_buf.rdep())
        elif base == 0:
            tok = sc.dma("pool", dst.t[:, col + 1:col + width],
                         zh_ap[c][:, 0:width - 1], writes=[dst],
                         extra=zh_buf.rdep())
            sc.op("dve", lambda e, col=col, dst=dst: e.memset(
                dst.t[:, col:col + 1], 0.0), writes=[dst])
        else:
            tok = sc.dma("pool", dst.t[:, col:col + width],
                         zh_ap[c][:, base - 1:base + width - 1], writes=[dst],
                         extra=zh_buf.rdep())
        zh_buf.read(tok)


def emit_shift_cast(sc, ctx, zh_ap, zh_buf, dst):
    for c in range(NCH):
        if c < 4:
            tok = sc.dma("pool", dst.t[:, c * T:(c + 1) * T], zh_ap[c],
                         writes=[dst], extra=zh_buf.rdep())
        else:
            tok = sc.dma("pool", dst.t[:, c * T + 1:(c + 1) * T],
                         zh_ap[c][:, 0:T - 1], writes=[dst],
                         extra=zh_buf.rdep())
            sc.op("dve", lambda e, c=c, dst=dst: e.memset(
                dst.t[:, c * T:c * T + 1], 0.0), writes=[dst])
        zh_buf.read(tok)


def emit_l1f(sc, ctx, al, s, src_ap, dst_ap, wd):
    pp = ctx["pp"]
    HB = 2816 // 128
    zh = wd["zh"]
    zh_buf = ctx["zh_buf"]
    top = al.scope()
    mkt = top.__enter__()
    zt = mkt("zt", NCH * TW)

    for tt in range(NTT):
        t0 = tt * TW
        xt = load_xtile(sc, ctx, src_ap, s, t0)
        cs = cast_x_tile(sc, ctx, xt)

        def bf_fn(c, cs=cs):
            return cs.t[:, c * TW:(c + 1) * TW], [cs]
        r_mu, r_rstd = ln_stats_tile(sc, ctx, bf_fn, want_mean=True)
        pmu = bcast_row(sc, ctx, r_mu)
        pa = bcast_row(sc, ctx, r_rstd)
        for c in range(NCH):
            tf = ctx["tmp_f"][c % 2]
            sc.op("dve", lambda e, c=c, tf=tf, xt=xt, pmu=pmu: e.tensor_tensor(
                tf.t[:], xt.t[:, c * TW:(c + 1) * TW], pmu.t[:, 0:TW],
                ALU.subtract), reads=[xt, pmu], writes=[tf])
            sc.op("dve", lambda e, c=c, tf=tf, pa=pa, zt=zt: e.tensor_tensor(
                zt.t[:, c * TW:(c + 1) * TW], tf.t[:], pa.t[:, 0:TW],
                ALU.mult), reads=[tf, pa], writes=[zt])
        store_ztile(sc, ctx, zh, zh_buf, t0, zt)

    for l in range(2):
        with al.scope() as mk2:
            qf = mk2("qf", 4 * T, BF16)
            kf = mk2("kf", 4 * T, BF16)
            vt = mk2("vt", 16 * 520, BF16)
            ao = mk2("ao", 4 * T, BF16)
            negc = mk2("negc", 1)
            scA1 = al.scope()
            mkA1 = scA1.__enter__()
            hshb = mkA1("hshb", NCH * T, BF16)
            qkw = [mkA1(f"qkw{i}", 8 * 128, BF16) for i in range(2)]
            qbt_b = [mkA1(f"qbt{i}", TW, BF16) for i in range(2)]
            sqb_b = [mkA1(f"sqb{i}", TW, BF16) for i in range(2)]
            fhi_b = [mkA1(f"fhi{i}", TW, BF16, parts=2) for i in range(2)]
            flo_b = [mkA1(f"flo{i}", TW, BF16, parts=2) for i in range(2)]
            frow_b = [mkA1(f"frow{i}", TW, parts=2) for i in range(2)]
            scp = mkA1("scp", 4)
            qrow = mkA1("qrow", 64, parts=1)
            krow = mkA1("krow", 64, parts=1)
            sc.dma("sp", scp.t[:, 0:1], wd["g_qs"][l], writes=[scp])
            sc.dma("sp", scp.t[:, 1:2], wd["g_qss"][l], writes=[scp])
            sc.dma("sp", scp.t[:, 2:3], wd["g_ks"][l], writes=[scp])
            sc.dma("sp", scp.t[:, 3:4], wd["g_kss"][l], writes=[scp])
            sc.dma("sp", qrow.t[:], wd["g_qs_row"][l], writes=[qrow])
            sc.dma("sp", krow.t[:], wd["g_ks_row"][l], writes=[krow])

            r_rec = ctx["r_rec"]
            sc.op("dve", lambda e: e.reduce_max(
                r_rec.t[0:1, 0:1], qrow.t[:], axis=AX.X,
                apply_absolute_value=True), reads=[qrow], writes=[r_rec])
            sc.op("dve", lambda e: e.reduce_max(
                r_rec.t[0:1, 1:2], krow.t[:], axis=AX.X,
                apply_absolute_value=True), reads=[krow], writes=[r_rec])
            sc.op("dve", lambda e: e.tensor_tensor(
                r_rec.t[0:1, 0:1], r_rec.t[0:1, 0:1], r_rec.t[0:1, 1:2],
                ALU.mult), writes=[r_rec])
            sc.op("dve", lambda e: e.tensor_scalar(
                r_rec.t[0:1, 0:1], r_rec.t[0:1, 0:1], -8.0, 40.0,
                ALU.mult, ALU.add), writes=[r_rec])
            sc.op("dve", lambda e: e.tensor_scalar_min(
                r_rec.t[0:1, 0:1], r_rec.t[0:1, 0:1], 0.0), writes=[r_rec])
            pbn = bcast_row(sc, ctx, r_rec, row_ap=r_rec.t[0:1, 0:1], width=1)
            sc.op("dve", lambda e, pbn=pbn: e.tensor_copy(
                negc.t[:], pbn.t[:, 0:1]), reads=[pbn], writes=[negc])

            emit_shift_cast(sc, ctx, zh, zh_buf, hshb)

            scTab = al.scope()
            mkTab = scTab.__enter__()
            tabs = mkTab("tabs", 2 * T)
            for part, ql in (("q", True), ("k", False)):
                t1n = "t1q" if ql else "t1k"
                t2n = "t2q" if ql else "t2k"
                sc.dma("sp", tabs.t[:, 0:T], wd[t1n], writes=[tabs])
                sc.dma("sp", tabs.t[:, T:2 * T], wd[t2n], writes=[tabs])
                sci = 0 if ql else 2
                for half, col in ((slice(0, T), sci), (slice(T, 2 * T), sci + 1)):
                    sc.op("dve", lambda e, half=half, col=col: e.tensor_scalar(
                        tabs.t[:, half], tabs.t[:, half], scp.t[:, col:col + 1],
                        None, ALU.mult), reads=[scp], writes=[tabs])
                dstf = qf if ql else kf
                for oc4 in range(4):
                    oc = oc4 if ql else 4 + oc4
                    st_ = qkw[oc4 % 2]
                    sc.dma("pool", st_.t[:], wd["g_wqkv_qk"][l][oc], writes=[st_])
                    for tt in range(NTT):
                        t0 = tt * TW
                        u = (oc4 * NTT + tt) % 2
                        qbt = qbt_b[u]
                        sqb = sqb_b[u]
                        fhi = fhi_b[u]
                        flo = flo_b[u]
                        fr = frow_b[u]
                        pb = pp.take()
                        items = [(pb.t[:, 0:TW],
                                  st_.t[:, kc * 128:(kc + 1) * 128],
                                  hshb.t[:, kc * T + t0: kc * T + t0 + TW],
                                  [st_, hshb]) for kc in range(NCH)]
                        sc.mm_group(pb, items)
                        sc.op("act", lambda e, pb=pb, qbt=qbt: e.activation(
                            qbt.t[:], pb.t[:, 0:TW], AF.Identity, bias=0.0,
                            scale=1.0), reads=[pb], writes=[qbt])
                        sc.op("act", lambda e, pb=pb, sqb=sqb: e.activation(
                            sqb.t[:], pb.t[:, 0:TW], AF.Square),
                            reads=[pb], writes=[sqb])
                        ph = pp.take(exclude=[pb])
                        sc.mm_group(ph, [(ph.t[0:2, 0:TW], ctx["hh_ind"].t[:],
                                          sqb.t[:], [sqb, ctx["hh_ind"]])])
                        sc.op("act", lambda e, ph=ph, fr=fr: e.activation(
                            fr.t[0:2, :], ph.t[0:2, 0:TW], AF.Ln),
                            reads=[ph], writes=[fr])
                        sc.op("act", lambda e, fr=fr: e.activation(
                            fr.t[0:2, :], fr.t[0:2, :], AF.Exp, scale=-0.5),
                            writes=[fr])
                        sc.op("dve", lambda e, fr=fr, ql=ql: e.tensor_scalar(
                            fr.t[0:2, :], fr.t[0:2, :], 1e12, 8.0 if ql else 1.0,
                            ALU.min, ALU.mult), writes=[fr])
                        sc.op("dve", lambda e, fr=fr, fhi=fhi: e.tensor_copy(
                            fhi.t[:], fr.t[0:2, :]), reads=[fr], writes=[fhi])
                        sc.op("dve", lambda e, fr=fr, fhi=fhi, flo=flo:
                              e.tensor_tensor(flo.t[:], fr.t[0:2, :], fhi.t[:],
                                              ALU.subtract),
                              reads=[fr, fhi], writes=[flo])
                        pf = pp.take(exclude=[pb, ph])
                        sc.mm_group(pf, [
                            (pf.t[:, 0:TW], ctx["hh_bcast"].t[:], fhi.t[:],
                             [fhi, ctx["hh_bcast"]]),
                            (pf.t[:, 0:TW], ctx["hh_bcast"].t[:], flo.t[:],
                             [flo])])
                        pr = pp.take(exclude=[pb, ph, pf])
                        sc.mm_group(pr, [(pr.t[:, 0:TW], ctx["rotm"].t[:],
                                          qbt.t[:], [qbt, ctx["rotm"]])])
                        tf0 = ctx["tmp_f"][2 * (u % 2)]
                        tf1 = ctx["tmp_f"][2 * (u % 2) + 1]
                        sc.op("dve", lambda e, tf0=tf0, t0=t0, qbt=qbt:
                              e.tensor_tensor(tf0.t[:], qbt.t[:],
                                              tabs.t[:, t0:t0 + TW], ALU.mult),
                              reads=[qbt, tabs], writes=[tf0])
                        sc.op("dve", lambda e, tf1=tf1, pr=pr, t0=t0:
                              e.tensor_tensor(tf1.t[:], pr.t[:, 0:TW],
                                              tabs.t[:, T + t0:T + t0 + TW],
                                              ALU.mult),
                              reads=[pr, tabs], writes=[tf1])
                        sc.op("dve", lambda e, tf0=tf0, tf1=tf1: e.tensor_tensor(
                            tf0.t[:], tf0.t[:], tf1.t[:], ALU.add),
                            reads=[tf1], writes=[tf0])
                        sc.op("dve", lambda e, tf0=tf0, pf=pf, dstf=dstf,
                              oc4=oc4, t0=t0: e.tensor_tensor(
                                  dstf.t[:, oc4 * T + t0: oc4 * T + t0 + TW],
                                  tf0.t[:], pf.t[:, 0:TW], ALU.mult),
                              reads=[tf0, pf], writes=[dstf])

            scTab.__exit__(None, None, None)

            vws = mkA1("vws", 8 * 512, BF16)
            for kc in range(NCH):
                sc.dma("pool", vws.t[:, kc * 512:(kc + 1) * 512],
                       wd["g_wqkv_v"][l][kc], writes=[vws])
            onesap = vt.t[:, 0:16 * 520].rearrange(
                "p (tc h o) -> p tc h o", tc=16, h=8)[:, :, :, 64:65]
            sc.op("dve", lambda e, onesap=onesap: e.memset(onesap, 1.0),
                  writes=[vt])
            for tc in range(16):
                pb = pp.take()
                items = [(pb.t[:, 0:512],
                          hshb.t[:, kc * T + tc * 128: kc * T + tc * 128 + 128],
                          vws.t[:, kc * 512:(kc + 1) * 512],
                          [vws, hshb]) for kc in range(NCH)]
                sc.mm_group(pb, items)
                d = vt.t[:, tc * 520: tc * 520 + 520].rearrange(
                    "p (h o) -> p h o", h=8)[:, :, 0:64]
                srcap = pb.t[:, 0:512].rearrange("p (h o) -> p h o", h=8)
                sc.op("dve", lambda e, d=d, srcap=srcap: e.tensor_copy(d, srcap),
                      reads=[pb], writes=[vt])

            scA1.__exit__(None, None, None)
            e_b = [mk2(f"el{i}", TW, BF16) for i in range(3)]
            em_b = [mk2(f"em{i}", TW, BF16) for i in range(3)]

            for h in range(8):
                rbase = (h % 2) * 64
                kcc = h // 2
                for w in range(4):
                    pav = pp.take()
                    kjs = [j for j in range(8) if (w - 1) * 4 + j >= 0]
                    nkj = len(kjs)
                    for b0 in range(0, nkj, 3):
                        batch = list(enumerate(kjs))[b0:b0 + 3]
                        stiles = []
                        for ji, j in batch:
                            tc = (w - 1) * 4 + j
                            pbs = pp.take(exclude=[pav])
                            kap = kf.t[rbase:rbase + 64,
                                       kcc * T + tc * 128: kcc * T + tc * 128 + 128]
                            qap = qf.t[rbase:rbase + 64,
                                       kcc * T + w * TW: kcc * T + (w + 1) * TW]
                            sc.mm_group(pbs, [(pbs.t[:, 0:TW], kap, qap, [kf, qf])])
                            stiles.append((ji, j, pbs))
                        ems = []
                        for ji, j, pbs in stiles:
                            eb = e_b[ji % 3]
                            sc.op("act", lambda e, pbs=pbs, eb=eb: e.activation(
                                eb.t[:], pbs.t[:, 0:TW], AF.Exp, bias=negc.t[:],
                                scale=1.0), reads=[pbs, negc], writes=[eb])
                            off = 896 - 128 * j
                            em = em_b[ji % 3]
                            sc.op("dve", lambda e, eb=eb, off=off, em=em: e.select(
                                em.t[:], ctx["mst1"].t[:, off:off + TW], eb.t[:],
                                ctx["zeros_bf"].t[:]),
                                reads=[eb, ctx["mst1"], ctx["zeros_bf"]],
                                writes=[em])
                            ems.append((ji, j, em))
                        for ji, j, em in ems:
                            tc = (w - 1) * 4 + j
                            vap = vt.t[:, tc * 520 + h * 65: tc * 520 + h * 65 + 65]
                            d = [] if ji > 0 else list(pav.buf.wdep())
                            d = d + em.buf.rdep() + vt.buf.rdep()
                            tok = sc.emit("pe", lambda e, vap=vap, em=em, pav=pav,
                                          f=(ji == 0), st=(ji == nkj - 1): e.matmul(
                                              pav.t[0:65, 0:TW], vap, em.t[:],
                                              start=f, stop=st),
                                          deps=d, inc=1)
                            em.buf.read(tok)
                            vt.buf.read(tok)
                            if ji == nkj - 1:
                                pav.buf.wrote(tok)
                    r_rec2 = ctx["r_recs"][(h * 4 + w) % 3]
                    sc.op("act", lambda e, pav=pav, r_rec2=r_rec2: e.activation(
                        r_rec2.t[:], pav.t[64:65, 0:TW], AF.Ln),
                        reads=[pav], writes=[r_rec2])
                    sc.op("act", lambda e, r_rec2=r_rec2: e.activation(
                        r_rec2.t[:], r_rec2.t[:], AF.Exp, scale=-1.0),
                        writes=[r_rec2])
                    pbc = bcast_row(sc, ctx, r_rec2)
                    tf0 = ctx["tmp_f"][(h * 4 + w) % 4]
                    sc.op("dve", lambda e, pav=pav, tf0=tf0: e.tensor_copy(
                        tf0.t[0:64, :], pav.t[0:64, 0:TW]),
                        reads=[pav], writes=[tf0])
                    aod = ao.t[rbase:rbase + 64,
                               kcc * T + w * TW: kcc * T + (w + 1) * TW]
                    sc.op("dve", lambda e, tf0=tf0, aod=aod, pbc=pbc:
                          e.tensor_tensor(aod, tf0.t[0:64, :],
                                          pbc.t[0:64, 0:TW], ALU.mult),
                          reads=[tf0, pbc], writes=[ao])

            wom = [mk2(f"wom{i}", 4 * 128, BF16) for i in range(2)]
            for tt in range(NTT):
                t0 = tt * TW
                load_ztile(sc, ctx, zh, zh_buf, t0, zt)
                for oc in range(NCH):
                    ws_ = wom[oc % 2]
                    sc.dma("pool", ws_.t[:], wd["g_wo"][l][oc], writes=[ws_])
                    pb = pp.take()
                    items = [(pb.t[:, 0:TW],
                              ws_.t[:, kc * 128:(kc + 1) * 128],
                              ao.t[:, kc * T + t0: kc * T + t0 + TW],
                              [ws_, ao]) for kc in range(4)]
                    sc.mm_group(pb, items)
                    sc.op("dve", lambda e, pb=pb, oc=oc, zt=zt:
                          e.scalar_tensor_tensor(
                              zt.t[:, oc * TW:(oc + 1) * TW], pb.t[:, 0:TW],
                              1.0, zt.t[:, oc * TW:(oc + 1) * TW],
                              ALU.mult, ALU.add), reads=[pb], writes=[zt])
                store_ztile(sc, ctx, zh, zh_buf, t0, zt)

        TT2 = 2 * TW
        with al.scope() as mk3:
            gb = mk3("gb", 26)
            b1s = mk3("b1s", 44)
            sc.dma("sp", gb.t[:, 0:8], wd["g_lng"][l], writes=[gb])
            sc.dma("sp", gb.t[:, 8:16], wd["g_lnb"][l], writes=[gb])
            sc.dma("sp", gb.t[:, 16:24], wd["g_b2"][l], writes=[gb])
            sc.dma("sp", b1s.t[:, 0:22], wd["g_b1a"][l], writes=[b1s])
            sc.dma("sp", b1s.t[:, 22:44], wd["g_b1g"][l], writes=[b1s])
            for t2 in range(T // TT2):
                base = t2 * TT2
                scX = al.scope()
                mkX = scX.__enter__()
                xs22 = mkX("xs22", NCH * TT2, BF16)
                scS = al.scope()
                mkS = scS.__enter__()
                hsh2 = mkS("hsh2", NCH * TT2, BF16)
                emit_shift_cast_half(sc, ctx, zh, zh_buf, base, TT2, hsh2)
                for sub in range(2):
                    t0r = sub * TW

                    def bfh(c, t0r=t0r, hsh2=hsh2):
                        return hsh2.t[:, c * TT2 + t0r:c * TT2 + t0r + TW], [hsh2]
                    r_mu, r_rstd = ln_stats_tile(sc, ctx, bfh, want_mean=True)
                    pmu = bcast_row(sc, ctx, r_mu)
                    pa = bcast_row(sc, ctx, r_rstd)
                    for c in range(NCH):
                        tf0, tf1 = ctx["tmp_f"][0], ctx["tmp_f"][1]
                        sc.op("dve", lambda e, c=c, tf0=tf0, t0r=t0r, pmu=pmu,
                              hsh2=hsh2: e.tensor_tensor(
                                  tf0.t[:],
                                  hsh2.t[:, c * TT2 + t0r:c * TT2 + t0r + TW],
                                  pmu.t[:, 0:TW], ALU.subtract),
                              reads=[hsh2, pmu], writes=[tf0])
                        sc.op("dve", lambda e, c=c, tf0=tf0, tf1=tf1, pa=pa:
                              e.scalar_tensor_tensor(tf1.t[:], tf0.t[:],
                                                     gb.t[:, c:c + 1],
                                                     pa.t[:, 0:TW],
                                                     ALU.mult, ALU.mult),
                              reads=[tf0, gb, pa], writes=[tf1])
                        sc.op("dve", lambda e, c=c, tf1=tf1, t0r=t0r:
                              e.tensor_scalar(
                                  xs22.t[:, c * TT2 + t0r:c * TT2 + t0r + TW],
                                  tf1.t[:], 1.0, gb.t[:, 8 + c:9 + c],
                                  ALU.mult, ALU.add),
                              reads=[tf1, gb], writes=[xs22])
                scS.__exit__(None, None, None)
                hb2 = mkX("hb2", HB * TT2, BF16)
                w1st = [mkX(f"w1st{i}", 8 * 128, BF16) for i in range(4)]
                for oc in range(HB):
                    sa = w1st[oc % 2]
                    sg_ = w1st[2 + oc % 2]
                    cdma(sc, ctx, sa, sa.t[:], wd["g_w1a"][l][oc],
                         ctx["wscr"]["g_w1a"][l][oc], ("g_w1a", l, oc))
                    cdma(sc, ctx, sg_, sg_.t[:], wd["g_w1g"][l][oc],
                         ctx["wscr"]["g_w1g"][l][oc], ("g_w1g", l, oc))
                    pa0 = pp.take()
                    pa1 = pp.take(exclude=[pa0])
                    items = [(sa.t[:, kc * 128:(kc + 1) * 128],
                              xs22.t[:, kc * TT2: kc * TT2 + TW],
                              xs22.t[:, kc * TT2 + TW: kc * TT2 + TT2],
                              [sa, xs22]) for kc in range(NCH)]
                    mm_group2(sc, pa0, pa0.t[:, 0:TW], pa1, pa1.t[:, 0:TW], items)
                    pg0 = pp.take(exclude=[pa0, pa1])
                    pg1 = pp.take(exclude=[pa0, pa1, pg0])
                    items = [(sg_.t[:, kc * 128:(kc + 1) * 128],
                              xs22.t[:, kc * TT2: kc * TT2 + TW],
                              xs22.t[:, kc * TT2 + TW: kc * TT2 + TT2],
                              [sg_, xs22]) for kc in range(NCH)]
                    mm_group2(sc, pg0, pg0.t[:, 0:TW], pg1, pg1.t[:, 0:TW], items)
                    for sub, pa_, pg_ in ((0, pa0, pg0), (1, pa1, pg1)):
                        tf0 = ctx["tmp_f"][2 * (oc % 2) + sub]
                        sc.op("act", lambda e, pg_=pg_, tf0=tf0, oc=oc:
                              e.activation(tf0.t[:], pg_.t[:, 0:TW], AF.Gelu,
                                           bias=b1s.t[:, 22 + oc:23 + oc],
                                           scale=1.0),
                              reads=[pg_, b1s], writes=[tf0])
                        sc.op("dve", lambda e, pa_=pa_, tf0=tf0, oc=oc, sub=sub:
                              e.scalar_tensor_tensor(
                                  hb2.t[:, oc * TT2 + sub * TW:
                                        oc * TT2 + sub * TW + TW],
                                  pa_.t[:, 0:TW], b1s.t[:, oc:oc + 1], tf0.t[:],
                                  ALU.add, ALU.mult),
                              reads=[pa_, b1s, tf0], writes=[hb2])
                ztt = mkX("ztt", NCH * TT2)
                w2st = [mkX(f"w2st{i}", HB * 128, BF16) for i in range(2)]
                load_zhalf(sc, ctx, zh, zh_buf, base, TT2, ztt)
                for oc in range(NCH):
                    st_ = w2st[oc % 2]
                    cdma(sc, ctx, st_, st_.t[:], wd["g_w2"][l][oc],
                         ctx["wscr"]["g_w2"][l][oc], ("g_w2", l, oc))
                    p0 = pp.take()
                    p1 = pp.take(exclude=[p0])
                    items = [(st_.t[:, kc * 128:(kc + 1) * 128],
                              hb2.t[:, kc * TT2: kc * TT2 + TW],
                              hb2.t[:, kc * TT2 + TW: kc * TT2 + TT2],
                              [st_, hb2]) for kc in range(HB)]
                    mm_group2(sc, p0, p0.t[:, 0:TW], p1, p1.t[:, 0:TW], items)
                    for sub, p_ in ((0, p0), (1, p1)):
                        tf1 = ctx["tmp_f"][2 * (oc % 2) + sub]
                        sc.op("dve", lambda e, p_=p_, oc=oc, tf1=tf1:
                              e.tensor_scalar(
                                  tf1.t[:], p_.t[:, 0:TW], 1.0,
                                  gb.t[:, 16 + oc:17 + oc],
                                  ALU.mult, ALU.add), reads=[p_, gb], writes=[tf1])
                        sc.op("dve", lambda e, oc=oc, sub=sub, tf1=tf1, ztt=ztt:
                              e.tensor_tensor(
                                  ztt.t[:, oc * TT2 + sub * TW:
                                        oc * TT2 + sub * TW + TW],
                                  ztt.t[:, oc * TT2 + sub * TW:
                                        oc * TT2 + sub * TW + TW],
                                  tf1.t[:], ALU.add), reads=[tf1], writes=[ztt])
                store_zhalf(sc, ctx, zh, zh_buf, base, TT2, ztt)
                scX.__exit__(None, None, None)

    for tt in range(NTT):
        t0 = tt * TW
        xt = load_xtile(sc, ctx, src_ap, s, t0)
        load_ztile(sc, ctx, zh, zh_buf, t0, zt)
        for c in range(NCH):
            sc.op("dve", lambda e, c=c, xt=xt, zt=zt: e.tensor_tensor(
                xt.t[:, c * TW:(c + 1) * TW], xt.t[:, c * TW:(c + 1) * TW],
                zt.t[:, c * TW:(c + 1) * TW], ALU.add),
                reads=[zt], writes=[xt])
        store_xtile(sc, ctx, dst_ap, s, t0, xt)
    top.__exit__(None, None, None)


EMITTERS["l1f"] = emit_l1f
